# version 06
# baseline (speedup 1.0000x reference)
import contextlib
import numpy as np
import ml_dtypes
import concourse.bass as bass
import concourse.mybir as mybir
from concourse.bass_utils import run_bass_kernel_spmd

F32 = mybir.dt.float32
BF16 = mybir.dt.bfloat16
I32 = mybir.dt.int32
AF = mybir.ActivationFunctionType
ALU = mybir.AluOpType
AX = mybir.AxisListType
BF = ml_dtypes.bfloat16

NCORES = 8
B = 4
T = 2048
D = 1024
HID = 3072
NHC = HID // 128
DC = D // 128
NTT = T // 128
NH = 16
EPS = 1e-5
QDEQ = 1.0 / 127.0
DOWN_SCALE = 0.5 * QDEQ * QDEQ

N_SUBLAYERS = 3
MM1_LOOKAHEAD = 3



def host_tables():
    t = {}
    half = 1.0 / np.power(10000.0, np.linspace(0.0, 1.0, 32))
    angle = np.repeat(half, 2)
    idx = np.arange(T, dtype=np.float64)
    sinT = np.sin(idx[:, None] * angle[None, :]).T.astype(np.float32)
    cosT = np.cos(idx[:, None] * angle[None, :]).T.astype(np.float32)
    t["sinT"] = np.ascontiguousarray(np.concatenate([sinT, sinT], 0)).astype(BF)
    t["cosT"] = np.ascontiguousarray(np.concatenate([cosT, cosT], 0)).astype(BF)
    s64 = np.zeros((64, 64), dtype=np.float32)
    for i in range(32):
        s64[2 * i + 1, 2 * i] = -1.0
        s64[2 * i, 2 * i + 1] = 1.0
    t["permT"] = np.ascontiguousarray(np.concatenate([s64, s64], 0)).astype(BF)
    t["ident"] = np.eye(128, dtype=np.float32).astype(BF)
    t["identf"] = np.eye(128, dtype=np.float32)
    dec = np.log(1.0 - 2.0 ** (-5.0 - np.arange(NH, dtype=np.float64)))
    n_ = np.arange(128, dtype=np.float64)
    bg3 = 0.125 * np.exp(-dec[None, :, None]
                         * (128.0 * np.arange(16)[None, None, :] + n_[:, None, None]))
    t["bg3"] = np.ascontiguousarray(bg3.reshape(128, NH * 16)).astype(np.float32)
    m_ = np.arange(128)
    trilmask = (m_[None, :] >= n_[:, None]).astype(np.float64)
    tril2 = np.stack([trilmask * 0.125 * np.exp(-dec[h] * n_)[:, None]
                      for h in range(NH)])
    t["tril2"] = tril2.astype(np.float32)
    mg = np.arange(T, dtype=np.float64)
    alpha = np.zeros((128, NH * 16), dtype=np.float64)
    for h in range(NH):
        csum = np.cumsum(np.exp(dec[h] * mg))
        am = np.exp(dec[h] * mg) / np.sqrt(csum)
        alpha[:, h * 16:(h + 1) * 16] = am.reshape(16, 128).T
    t["alpha"] = alpha.astype(np.float32)
    t["diag_scale"] = np.exp(-dec[:, None] * 128.0 * np.arange(16)[None, :])
    return t


def retile_weights(ins):
    out = {}
    wg = np.asarray(ins["L0_ffn_w_gate"])
    wu = np.asarray(ins["L0_ffn_w_up"])
    wd = np.asarray(ins["L0_ffn_w_down"])

    def tile_gate(w):
        return np.ascontiguousarray(w.reshape(NHC, 128, DC, 128).transpose(0, 3, 2, 1))

    out["wgu_t"] = np.ascontiguousarray(
        np.stack([tile_gate(wg), tile_gate(wu)], axis=1))
    out["wd_t"] = np.ascontiguousarray(
        wd.reshape(D, NHC, 128).transpose(1, 2, 0))

    def tile_qk(w):
        return np.ascontiguousarray(
            w.reshape(DC, 128, NH // 2, 128).transpose(2, 1, 0, 3))

    out["qw_t"] = tile_qk(np.asarray(ins["L0_attn_q_w"]))
    out["kw_t"] = tile_qk(np.asarray(ins["L0_attn_k_w"]))

    def tile_vg(w):
        return np.ascontiguousarray(
            w.reshape(DC, 128, NH // 2, 256).transpose(2, 1, 0, 3))

    out["vw_t"] = tile_vg(np.asarray(ins["L0_attn_v_w"]))
    out["gw_t"] = tile_vg(np.asarray(ins["L0_attn_g_w"]))
    out["ow_t"] = np.ascontiguousarray(np.asarray(ins["L0_attn_o_w"])).reshape(NH, 128, D)
    for k in ("q_b", "k_b", "v_b", "g_b", "o_b"):
        out[k] = np.ascontiguousarray(np.asarray(ins["L0_attn_" + k]))
    return out



class Sem:
    def __init__(self, stack, nc, name):
        self.h = stack.enter_context(nc.semaphore(name))
        self.n = 0

    def bump(self, k=1):
        self.n += k
        return self.n


class Prog:
    ENGINES = ("sync", "tensor", "vector", "scalar", "gpsimd")

    def __init__(self):
        self.streams = {e: [] for e in self.ENGINES}

    def emit(self, eng, fn):
        self.streams[eng].append(fn)

    def realize(self, block):
        for name in self.ENGINES:
            fns = self.streams[name]
            getattr(block, name)(lambda eng, fns=fns: [f(eng) for f in fns])


def build_program(tabs):
    nc = bass.Bass("TRN2", target_bir_lowering=False)
    stack = contextlib.ExitStack()

    x_in = nc.dram_tensor("x_in", [B, T, D], F32, kind="ExternalInput").ap()
    x_out = nc.dram_tensor("x_out", [B, T, D], F32, kind="ExternalOutput").ap()
    x_scr = nc.dram_tensor("x_scr", [B, T, D], F32, kind="Internal").ap()

    wgu_t = nc.dram_tensor("wgu_t", [NHC, 2, 128, DC, 128], I32, kind="ExternalInput").ap()
    wd_t = nc.dram_tensor("wd_t", [NHC, 128, D], I32, kind="ExternalInput").ap()
    qw_t = nc.dram_tensor("qw_t", [NH // 2, 128, DC, 128], F32, kind="ExternalInput").ap()
    kw_t = nc.dram_tensor("kw_t", [NH // 2, 128, DC, 128], F32, kind="ExternalInput").ap()
    vw_t = nc.dram_tensor("vw_t", [NH // 2, 128, DC, 256], F32, kind="ExternalInput").ap()
    gw_t = nc.dram_tensor("gw_t", [NH // 2, 128, DC, 256], F32, kind="ExternalInput").ap()
    ow_t = nc.dram_tensor("ow_t", [NH, 128, D], F32, kind="ExternalInput").ap()
    q_b = nc.dram_tensor("q_b", [D], F32, kind="ExternalInput").ap()
    k_b = nc.dram_tensor("k_b", [D], F32, kind="ExternalInput").ap()
    v_b = nc.dram_tensor("v_b", [2 * D], F32, kind="ExternalInput").ap()
    g_b = nc.dram_tensor("g_b", [2 * D], F32, kind="ExternalInput").ap()
    o_b = nc.dram_tensor("o_b", [D], F32, kind="ExternalInput").ap()

    c_sinT = nc.dram_tensor("c_sinT", [128, T], BF16, kind="ExternalInput").ap()
    c_cosT = nc.dram_tensor("c_cosT", [128, T], BF16, kind="ExternalInput").ap()
    c_permT = nc.dram_tensor("c_permT", [128, 64], BF16, kind="ExternalInput").ap()
    c_ident = nc.dram_tensor("c_ident", [128, 128], BF16, kind="ExternalInput").ap()
    c_identf = nc.dram_tensor("c_identf", [128, 128], F32, kind="ExternalInput").ap()
    c_bg3 = nc.dram_tensor("c_bg3", [128, NH * 16], F32, kind="ExternalInput").ap()
    c_tril2 = nc.dram_tensor("c_tril2", [NH, 128, 128], F32, kind="ExternalInput").ap()
    c_alpha = nc.dram_tensor("c_alpha", [128, NH * 16], F32, kind="ExternalInput").ap()

    s_wgu = nc.dram_tensor("s_wgu", [NHC, 2, 128, DC * 128], BF16, kind="Internal").ap()
    s_wd = nc.dram_tensor("s_wd", [NHC, 128, D], BF16, kind="Internal").ap()
    s_qw = nc.dram_tensor("s_qw", [NH // 2, 128, DC * 128], BF16, kind="Internal").ap()
    s_kw = nc.dram_tensor("s_kw", [NH // 2, 128, DC * 128], BF16, kind="Internal").ap()
    s_vw = nc.dram_tensor("s_vw", [NH // 2, 128, DC * 256], BF16, kind="Internal").ap()
    s_gw = nc.dram_tensor("s_gw", [NH // 2, 128, DC * 256], BF16, kind="Internal").ap()
    s_ow_d = nc.dram_tensor("s_ow", [NH, 128, D], BF16, kind="Internal").ap()

    def sb(name, shape, dt):
        return stack.enter_context(nc.sbuf_tensor(name, shape, dt))

    LNT = sb("LNT", [128, DC * T], BF16)
    arenaA = sb("arenaA", [128, 16 * T], BF16)
    arenaB = sb("arenaB", [128, NHC * D], BF16)
    wgu_sb = sb("wgu_sb", [128, 2 * 2048], BF16)
    sg_sb = sb("sg_sb", [128, 1024], BF16)
    xin_sb = sb("xin_sb", [128, 2 * D], F32)
    xrs_sb = sb("xrs_sb", [128, 2 * D], F32)
    xpo_sb = sb("xpo_sb", [128, 2 * D], F32)
    st_s1 = sb("st_s1", [128, 16], F32)
    st_s2 = sb("st_s2", [128, 16], F32)
    st_u = sb("st_u", [128, 16], F32)
    st_v = sb("st_v", [128, 16], F32)
    st_r = sb("st_r", [128, 16], F32)
    st_nb = sb("st_nb", [128, 16], F32)
    sinT = sb("sinT", [128, T], BF16)
    cosT = sb("cosT", [128, T], BF16)
    permT = sb("permT", [128, 64], BF16)
    ident = sb("ident", [128, 128], BF16)
    identf = sb("identf", [128, 128], F32)
    bg3 = sb("bg3", [128, NH * 16], F32)
    tril2 = sb("tril2", [128, NH * 128], F32)
    alpha = sb("alpha", [128, NH * 16], F32)
    bias_q = sb("bias_q", [128, DC], F32)
    bias_k = sb("bias_k", [128, DC], F32)
    gb_f = sb("gb_f", [128, NH], F32)
    vb_rep = sb("vb_rep", [128, 2 * D], BF16)
    ob_rep = sb("ob_rep", [128, D], BF16)
    bstage = sb("bstage", [128, 128], F32)
    ep_rs = sb("ep_rs", [128, 16], F32)
    ep_d = sb("ep_d", [128, 16], F32)
    ep_beta = sb("ep_beta", [128, 16], F32)
    ep_m1 = sb("ep_m1", [128, 16], F32)
    ep_m2 = sb("ep_m2", [128, 16], F32)
    ep_sc = sb("ep_sc", [128, 16], F32)
    ep_bi = sb("ep_bi", [128, 16], F32)
    c_eps = sb("c_eps", [128, 1], F32)

    junk_sb2 = [arenaA[:, 28672:30720].bitcast(F32)[:, 0:D],
                arenaA[:, 30720:32768].bitcast(F32)[:, 0:D]]

    banks = [stack.enter_context(nc.psum_tensor(f"pb{i}", [128, 512], F32))
             for i in range(8)]

    def bank_bf16(i):
        return banks[i][:].bitcast(BF16)

    s_dc = Sem(stack, nc, "s_dc")
    s_dc2 = Sem(stack, nc, "s_dc2")
    s_bs = Sem(stack, nc, "s_bs")
    s_cl = [Sem(stack, nc, f"s_cl{i}") for i in range(2)]
    s_cs = [Sem(stack, nc, f"s_cs{i}") for i in range(2)]
    s_wg = [Sem(stack, nc, f"s_wg{i}") for i in range(2)]
    s_wdT = Sem(stack, nc, "s_wdT")
    s_dx = [Sem(stack, nc, f"s_dx{i}") for i in range(2)]
    s_dr = [Sem(stack, nc, f"s_dr{i}") for i in range(2)]
    s_st = [Sem(stack, nc, f"s_st{i}") for i in range(2)]
    s_wq = Sem(stack, nc, "s_wq")
    s_wk = Sem(stack, nc, "s_wk")
    s_wv = Sem(stack, nc, "s_wv")
    s_wg2 = Sem(stack, nc, "s_wg2")
    s_ow = Sem(stack, nc, "s_ow")
    s_pe = Sem(stack, nc, "s_pe")
    s_vv = Sem(stack, nc, "s_vv")
    s_aa = Sem(stack, nc, "s_aa")
    s_gp = Sem(stack, nc, "s_gp")

    P = Prog()

    def on(eng, fn, waits=(), inc=None, inc_n=1):
        waits = tuple((s, v) for (s, v) in waits if s is not None)
        tok = inc.bump(inc_n) if inc is not None else None

        def f(e, fn=fn, waits=waits, inc=inc, inc_n=inc_n):
            for (s, v) in waits:
                e.wait_ge(s.h, v)
            r = fn(e)
            if inc is not None:
                r.then_inc(inc.h, inc_n)
        P.emit(eng, f)
        return tok

    def dma(queue, dst, src, sem, waits=()):
        waits = tuple((s, v) for (s, v) in waits if s is not None)
        tok = sem.bump(16)

        def f(e, dst=dst, src=src, sem=sem, waits=waits):
            for (s, v) in waits:
                e.wait_ge(s.h, v)
            e.dma_start(out=dst, in_=src).then_inc(sem.h, 16)
        P.emit(queue, f)
        return tok

    bank_free = [None] * 8

    def bf_waits(i):
        return list(bank_free[i]) if bank_free[i] is not None else []

    dma("sync", sinT[:], c_sinT, s_dc)
    dma("sync", cosT[:], c_cosT, s_dc)
    dma("sync", permT[:], c_permT, s_dc)
    dma("sync", ident[:], c_ident, s_dc)
    dma("sync", identf[:], c_identf, s_dc)
    dma("sync", bg3[:], c_bg3, s_dc)
    dma("sync", alpha[:], c_alpha, s_dc)
    for h0 in range(0, NH, 4):
        dma("sync", tril2[:].rearrange("p (h m) -> p h m", h=NH)[:, h0:h0 + 4, :],
            c_tril2[h0:h0 + 4].transpose([1, 0, 2]), s_dc)
    consts_tok = s_dc.n
    dma("gpsimd", vb_rep[:], v_b[None, :].broadcast_to([128, 2 * D]), s_dc2)
    dma("gpsimd", ob_rep[:], o_b[None, :].broadcast_to([128, D]), s_dc2)
    consts2_tok = s_dc2.n

    t_eps = on("vector", lambda v: v.memset(c_eps[:], EPS), inc=s_vv)
    on("scalar", lambda s: s.wait_ge(s_vv.h, t_eps))
    t_z = on("vector", lambda v: v.memset(bstage[:], 0.0), inc=s_vv)
    small_b = [(q_b, DC, bias_q), (k_b, DC, bias_k), (g_b, NH, gb_f)]
    for i, (src, rows, dst) in enumerate(small_b):
        t_b = dma("sync", bstage[0:rows, :], src.rearrange("(c p) -> c p", p=128),
                  s_bs, waits=[(s_vv, s_vv.n)])
        t_t = on("tensor", lambda t: t.transpose(banks[0][:, 0:128], bstage[:], identf[:]),
                 waits=[(s_bs, t_b), (s_dc, consts_tok)] + bf_waits(0), inc=s_pe)
        t_c = on("vector", lambda v, dst=dst, rows=rows: v.tensor_copy(
            dst[:], banks[0][:, 0:rows]), waits=[(s_pe, t_t)], inc=s_vv)
        bank_free[0] = [(s_vv, t_c)]
        if i < len(small_b) - 1:
            on("vector", lambda v, rows=rows: v.memset(bstage[0:rows, :], 0.0), inc=s_vv)
    bias_done = s_vv.n

    cast_jobs = []
    for hc in range(NHC):
        cast_jobs.append(("wgu", hc, wgu_t[hc].rearrange("g p c j -> p g (c j)"),
                          s_wgu[hc].rearrange("g p f -> p g f"), 2048))
    for hc in range(NHC):
        cast_jobs.append(("wd", hc, wd_t[hc], s_wd[hc], 1024))
    for pr in range(NH // 2):
        cast_jobs.append(("qw", pr, qw_t[pr].rearrange("p c j -> p (c j)"), s_qw[pr], 1024))
        cast_jobs.append(("kw", pr, kw_t[pr].rearrange("p c j -> p (c j)"), s_kw[pr], 1024))
    for pr in range(NH // 2):
        cast_jobs.append(("vw", pr, vw_t[pr].rearrange("p c j -> p (c j)"), s_vw[pr], 2048))
        cast_jobs.append(("gw", pr, gw_t[pr].rearrange("p c j -> p (c j)"), s_gw[pr], 2048))
    for h in range(NH):
        cast_jobs.append(("ow", h, ow_t[h], s_ow_d[h], 1024))

    cast_tok = {}
    store_toks = []
    for i, (kind, idx, src, dst, width) in enumerate(cast_jobs):
        slot = i % 2
        sl = arenaA[:, 24576 + slot * 2048: 24576 + slot * 2048 + width]
        if kind == "wgu":
            sl = sl.rearrange("p (g f) -> p g f", g=2)
        w = []
        if i >= 2:
            w.append((s_cs[slot], store_toks[i - 2]))
        g_tok = dma("gpsimd", sl, src, s_cl[slot], waits=w)
        st = dma("sync", dst, sl, s_cs[slot], waits=[(s_cl[slot], g_tok)])
        store_toks.append(st)
        cast_tok[(kind, idx)] = (s_cs[slot], st)
    cast_done = [(s_cs[0], s_cs[0].n), (s_cs[1], s_cs[1].n)]

    apply_toks = [None, None]
    xpo_free = [None, None]
    ln_tile_tok = [0] * NTT
    store_tile_tok = [None] * NTT
    lnv_free = [None, None]
    sg_free = [None, None]

    def emit_ln(src_x, wait_store_tok=None, tail_guard=None):
        rd_toks = {}
        for tt in range(NTT):
            sl = tt % 2
            w = [wait_store_tok[tt]] if wait_store_tok is not None else []
            if tt >= 2:
                w.append(rd_toks[tt - 2][0])
                w.append(rd_toks[tt - 2][1])
            xt_tok = dma("sync", xin_sb[:, sl * D:(sl + 1) * D],
                         src_x[tt * 128:(tt + 1) * 128, :], s_dx[sl], waits=w)
            xin = xin_sb[:, sl * D:(sl + 1) * D]
            t_r = on("vector", lambda v, xin=xin, tt=tt: v.reduce_sum(
                st_s1[:, tt:tt + 1], xin, axis=AX.X),
                waits=[(s_dx[sl], xt_tok)], inc=s_vv)
            w_sq = [(s_dx[sl], xt_tok)]
            if tt == 0 and tail_guard is not None:
                w_sq.append(tail_guard)
            t_q = on("scalar", lambda s, xin=xin, tt=tt: s.activation(
                junk_sb2[tt % 2], xin, AF.Square, accum_out=st_s2[:, tt:tt + 1]),
                waits=w_sq, inc=s_aa)
            rd_toks[tt] = ((s_vv, t_r), (s_aa, t_q))
        last_red = rd_toks[NTT - 1][0][1]
        last_sq = rd_toks[NTT - 1][1][1]
        t_u = on("vector", lambda v: v.scalar_tensor_tensor(
            st_u[:], st_s1[:], 1.0 / D, st_s1[:], ALU.mult, ALU.mult),
            waits=[(s_vv, last_red)], inc=s_vv)
        t_vd = on("vector", lambda v: v.tensor_sub(st_v[:], st_s2[:], st_u[:]),
                  waits=[(s_vv, t_u), (s_aa, last_sq)], inc=s_vv)
        t_l = on("scalar", lambda s: s.activation(
            st_r[:], st_v[:], AF.Ln, scale=1.0 / D, bias=c_eps[:, 0:1]),
            waits=[(s_vv, t_vd)], inc=s_aa)
        t_e = on("scalar", lambda s: s.activation(
            st_r[:], st_r[:], AF.Exp, scale=-0.5),
            waits=[(s_aa, t_l)], inc=s_aa)
        t_nb = on("vector", lambda v: v.scalar_tensor_tensor(
            st_nb[:], st_s1[:], -1.0 / D, st_r[:], ALU.mult, ALU.mult),
            waits=[(s_aa, t_e)], inc=s_vv)
        last_evac = None
        for tt in range(NTT):
            sl = tt % 2
            w = []
            if apply_toks[sl] is not None:
                w.append(apply_toks[sl])
            if tt < 2:
                w.append(rd_toks[14 + tt][0])
                w.append(rd_toks[14 + tt][1])
            if wait_store_tok is not None:
                w.append(wait_store_tok[tt])
            xt_tok = dma("sync", xin_sb[:, sl * D:(sl + 1) * D],
                         src_x[tt * 128:(tt + 1) * 128, :], s_dx[sl], waits=w)
            xin = xin_sb[:, sl * D:(sl + 1) * D]
            lnv = xpo_sb[:, sl * D:(sl + 1) * D].bitcast(BF16)[:, 0:D]
            w_ap = [(s_vv, t_nb), (s_dx[sl], xt_tok)]
            if xpo_free[sl] is not None:
                w_ap.append(xpo_free[sl])
            if lnv_free[sl] is not None:
                w_ap.append(lnv_free[sl])
            t_ap = on("vector", lambda v, xin=xin, lnv=lnv, tt=tt: v.tensor_scalar(
                lnv, xin, st_r[:, tt:tt + 1], st_nb[:, tt:tt + 1], ALU.mult, ALU.add),
                waits=w_ap, inc=s_vv)
            apply_toks[sl] = (s_vv, t_ap)
            bk = 6 + (tt % 2)
            tokp = None
            for c in range(DC):
                w2 = []
                if c == 0:
                    w2 = [(s_vv, t_ap)] + bf_waits(bk)
                tokp = on("tensor", lambda t, bk=bk, c=c, lnv=lnv: t.transpose(
                    bank_bf16(bk)[:, c * 128:(c + 1) * 128],
                    lnv[:, c * 128:(c + 1) * 128], ident[:]),
                    waits=w2, inc=s_pe if c == DC - 1 else None)
            lnv_free[sl] = (s_pe, tokp)
            dst = LNT[:].rearrange("p (c t) -> p c t", c=DC)[:, :, tt * 128:(tt + 1) * 128]
            t_ev = on("vector", lambda v, bk=bk, dst=dst: v.tensor_copy(
                dst, bank_bf16(bk)[:].rearrange("p (c t) -> p c t", c=DC)),
                waits=[(s_pe, tokp)], inc=s_vv)
            bank_free[bk] = [(s_vv, t_ev)]
            ln_tile_tok[tt] = t_ev
            last_evac = t_ev
        return last_evac

    H_view = arenaA
    wdT_sb = arenaB
    state = {"wgu_slot_free": [None, None], "arenaB_free": None,
             "resid_free": [None, None], "yT_read_done": None,
             "wqk_free": None, "wvg_free": None,
             "junk2_free": None, "junk2_guard": None}

    def emit_ffn(src_x, dst_x, wait_src_store=None, tail_guard=None):
        def load_wgu(hc):
            slot = hc % 2
            w = [cast_tok[("wgu", hc)]]
            if state["wgu_slot_free"][slot] is not None:
                w.append((s_pe, state["wgu_slot_free"][slot]))
            return dma("sync",
                       wgu_sb[:, slot * 2048:(slot + 1) * 2048].rearrange(
                           "p (g f) -> p g f", g=2),
                       s_wgu[hc].rearrange("g p f -> p g f"), s_wg[slot], waits=w)

        pre0 = {0: load_wgu(0), 1: load_wgu(1)}
        wd_tok = None
        for hc in range(NHC):
            w = [cast_tok[("wd", hc)]]
            if hc == 0 and state["arenaB_free"] is not None:
                w.extend(state["arenaB_free"])
            wd_tok = dma("sync", wdT_sb[:, hc * 1024:(hc + 1) * 1024],
                         s_wd[hc], s_wdT, waits=w)
        ln_tok = emit_ln(src_x, wait_store_tok=wait_src_store, tail_guard=tail_guard)
        h_done_halves = []
        for half in range(2):
            t0 = half * 1024
            pre = pre0 if half == 0 else {0: load_wgu(0), 1: load_wgu(1)}
            for hc in range(NHC):
                slot = hc % 2
                wtok = pre[hc] if hc in pre else load_wgu(hc)
                swg = s_wg[slot]
                for tc in range(2):
                    gbk, ubk = 2 * tc, 2 * tc + 1
                    rhs = LNT[:].rearrange("p (c t) -> p c t", c=DC)[
                        :, :, t0 + tc * 512: t0 + (tc + 1) * 512]
                    for c in range(DC):
                        wz = []
                        if c == 0:
                            wz = [(swg, wtok)] + bf_waits(gbk)
                            if hc == 0:
                                wz.append((s_vv, ln_tile_tok[(t0 + tc * 512 + 511) // 128]))
                        on("tensor", lambda t, c=c, gbk=gbk, rhs=rhs, slot=slot: t.matmul(
                            banks[gbk][:],
                            wgu_sb[:, slot * 2048 + c * 128: slot * 2048 + (c + 1) * 128],
                            rhs[:, c, :], start=(c == 0), stop=(c == DC - 1)),
                            waits=wz, inc=s_pe if c == DC - 1 else None)
                    g_tok = s_pe.n
                    for c in range(DC):
                        wz = []
                        if c == 0:
                            wz = bf_waits(ubk)
                        on("tensor", lambda t, c=c, ubk=ubk, rhs=rhs, slot=slot: t.matmul(
                            banks[ubk][:],
                            wgu_sb[:, slot * 2048 + 1024 + c * 128: slot * 2048 + 1024 + (c + 1) * 128],
                            rhs[:, c, :], start=(c == 0), stop=(c == DC - 1)),
                            waits=wz, inc=s_pe if c == DC - 1 else None)
                    u_tok = s_pe.n
                    sgv = sg_sb[:, tc * 512:(tc + 1) * 512]
                    w_sg = [(s_pe, g_tok)]
                    if sg_free[tc] is not None:
                        w_sg.append(sg_free[tc])
                    a_tok = on("scalar", lambda s, gbk=gbk, sgv=sgv: s.activation(
                        sgv, banks[gbk][:], AF.Silu, scale=QDEQ),
                        waits=w_sg, inc=s_aa)
                    bank_free[gbk] = [(s_aa, a_tok)]
                    hdst = H_view[:, hc * 1024 + tc * 512: hc * 1024 + (tc + 1) * 512]
                    v_tok = on("vector", lambda v, sgv=sgv, ubk=ubk, hdst=hdst: v.tensor_tensor(
                        hdst, sgv, banks[ubk][:], ALU.mult),
                        waits=[(s_pe, u_tok), (s_aa, a_tok)], inc=s_vv)
                    bank_free[ubk] = [(s_vv, v_tok)]
                    sg_free[tc] = (s_vv, v_tok)
                state["wgu_slot_free"][slot] = s_pe.n
            h_done = s_vv.n
            for ttl in range(8):
                tt = half * 8 + ttl
                sl = tt % 2
                w = [wait_src_store[tt]] if wait_src_store is not None else []
                if state["resid_free"][sl] is not None:
                    w.append(state["resid_free"][sl])
                r_tok = dma("sync", xrs_sb[:, sl * D:(sl + 1) * D],
                            src_x[tt * 128:(tt + 1) * 128, :], s_dr[sl], waits=w)
                e_tok = None
                for dc in range(2):
                    bk = (4 if ttl % 2 == 0 else 6) + dc
                    for hc in range(NHC):
                        wz = []
                        if hc == 0:
                            wz = [(s_wdT, wd_tok), (s_vv, h_done)] + bf_waits(bk)
                        on("tensor", lambda t, hc=hc, bk=bk, ttl=ttl, dc=dc: t.matmul(
                            banks[bk][:],
                            H_view[:, hc * 1024 + ttl * 128: hc * 1024 + (ttl + 1) * 128],
                            wdT_sb[:, hc * 1024 + dc * 512: hc * 1024 + (dc + 1) * 512],
                            start=(hc == 0), stop=(hc == NHC - 1)),
                            waits=wz, inc=s_pe if hc == NHC - 1 else None)
                    d_tok = s_pe.n
                    wz = [(s_pe, d_tok), (s_dr[sl], r_tok)]
                    if xpo_free[sl] is not None:
                        wz.append(xpo_free[sl])
                    if lnv_free[sl] is not None:
                        wz.append(lnv_free[sl])
                        if dc == 1:
                            lnv_free[sl] = None
                    e_tok = on("vector", lambda v, bk=bk, sl=sl, dc=dc: v.scalar_tensor_tensor(
                        xpo_sb[:, sl * D + dc * 512: sl * D + (dc + 1) * 512],
                        banks[bk][:], DOWN_SCALE,
                        xrs_sb[:, sl * D + dc * 512: sl * D + (dc + 1) * 512],
                        ALU.mult, ALU.add), waits=wz, inc=s_vv)
                    bank_free[bk] = [(s_vv, e_tok)]
                state["resid_free"][sl] = (s_vv, e_tok)
                st_tok = dma("sync", dst_x[tt * 128:(tt + 1) * 128, :],
                             xpo_sb[:, sl * D:(sl + 1) * D], s_st[sl],
                             waits=[(s_vv, e_tok)])
                xpo_free[sl] = (s_st[sl], st_tok)
                store_tile_tok[tt] = (s_st[sl], st_tok)
            h_done_halves.append(h_done)
        state["arenaB_free"] = [(s_pe, s_pe.n)]
        return list(store_tile_tok)


    OUT_BANKS = (6, 7, 2, 3)
    S_BANKS = (4, 5, 0, 1)
    OFF_V, OFF_SG, OFF_QT, OFF_KT = 0, 4128, 6176, 8224
    OFF_T1, OFF_T2, OFF_PP, OFF_OA = 10272, 12320, 14368, 15392
    OFF_WQK, OFF_WVG = 17456, 19504
    dsc = tabs["diag_scale"]

    def emit_attn(src_x, dst_x, wait_src_store):
        yT = arenaA
        VALL = arenaB[:, OFF_V:OFF_V + 4128]
        V3 = VALL.rearrange("p (hl z) -> p hl z", hl=2)
        SG = arenaB[:, OFF_SG:OFF_SG + 2048]
        QT = arenaB[:, OFF_QT:OFF_QT + 2048]
        KT = arenaB[:, OFF_KT:OFF_KT + 2048]
        T1 = arenaB[:, OFF_T1:OFF_T1 + 2048]
        T2 = arenaB[:, OFF_T2:OFF_T2 + 2048]
        PP = arenaB[:, OFF_PP:OFF_PP + 1024]
        OA = arenaB[:, OFF_OA:OFF_OA + 2064].bitcast(F32)
        OA3 = OA.rearrange("p (a b) -> p a b", b=129)
        WQK = arenaB[:, OFF_WQK:OFF_WQK + 2048]
        WVG = arenaB[:, OFF_WVG:OFF_WVG + 4096]
        junk3 = xrs_sb[:, 0:D].rearrange("p (a b) -> p a b", a=8)

        state["junk2_guard"] = [t for t in state["resid_free"] if t is not None]
        ln_tok = emit_ln(src_x, wait_store_tok=wait_src_store,
                         tail_guard=state["yT_read_done"])
        abf = state["arenaB_free"] or []
        t_ones = on("vector", lambda v: v.memset(VALL, 1.0),
                    waits=abf + [(s_dc2, consts2_tok)], inc=s_vv)

        oa_free = None
        gn_free = None
        sg_free2 = None
        last_mm2 = None
        last_ymul = None
        for pr in range(NH // 2):
            h0, h1 = 2 * pr, 2 * pr + 1
            w = [cast_tok[("qw", pr)]]
            if state["wqk_free"] is not None:
                w.append(state["wqk_free"])
            elif abf:
                w.extend(abf)
            tq = dma("sync", WQK[:, 0:1024], s_qw[pr], s_wq, waits=w)
            w2k = [cast_tok[("kw", pr)]]
            if state["wqk_free"] is not None:
                w2k.append(state["wqk_free"])
            tk = dma("sync", WQK[:, 1024:2048], s_kw[pr], s_wk, waits=w2k)
            w = [cast_tok[("vw", pr)]]
            if state["wvg_free"] is not None:
                w.append(state["wvg_free"])
            elif abf:
                w.extend(abf)
            tv = dma("sync", WVG[:, 0:2048], s_vw[pr], s_wv, waits=w)
            w2g = [cast_tok[("gw", pr)]]
            if state["wvg_free"] is not None:
                w2g.append(state["wvg_free"])
            tg = dma("sync", WVG[:, 2048:4096], s_gw[pr], s_wg2, waits=w2g)

            vev_toks = []
            for tt in range(NTT):
                bkv = tt % 2
                for c in range(DC):
                    wz = []
                    if c == 0:
                        wz = [(s_wv, tv)] + bf_waits(bkv)
                        if pr == 0 and tt < 2:
                            wz.append((s_vv, ln_tok))
                    on("tensor", lambda t, c=c, bkv=bkv, tt=tt: t.matmul(
                        banks[bkv][:, 0:256],
                        LNT[:, c * 2048 + tt * 128: c * 2048 + (tt + 1) * 128],
                        WVG[:, c * 256:(c + 1) * 256],
                        start=(c == 0), stop=(c == DC - 1)),
                        waits=wz, inc=s_pe if c == DC - 1 else None)
                v_tok = s_pe.n
                wz = [(s_pe, v_tok), (s_vv, t_ones)]
                t_ve = on("vector", lambda v, bkv=bkv, tt=tt, pr=pr: v.tensor_tensor(
                    V3[:, :, tt * 129: tt * 129 + 128],
                    banks[bkv][:, 0:256].rearrange("p (hl j) -> p hl j", hl=2),
                    vb_rep[:, pr * 256:(pr + 1) * 256].rearrange("p (hl j) -> p hl j", hl=2),
                    ALU.add), waits=wz, inc=s_vv)
                bank_free[bkv] = [(s_vv, t_ve)]
                vev_toks.append(t_ve)

            def qk_rotary(which, wtok, wsem, dstT, bias_ap, t2_guard=None):
                toks = {}
                proj_tok = {}

                def proj(tc):
                    bkq = 2 + tc % 2
                    wz0 = [(wsem, wtok)] + bf_waits(bkq)
                    if pr == 0 and tc < 2 and which == 0:
                        wz0.append((s_vv, ln_tok))
                    if pr == 0 and tc == 0 and which == 0:
                        wz0.append((s_dc, consts_tok))
                    for c in range(DC):
                        on("tensor", lambda t, c=c, bkq=bkq, tc=tc, which=which: t.matmul(
                            banks[bkq][:],
                            WQK[:, which * 1024 + c * 128: which * 1024 + (c + 1) * 128],
                            LNT[:].rearrange("p (c t) -> p c t", c=DC)[:, c, tc * 512:(tc + 1) * 512],
                            start=(c == 0), stop=(c == DC - 1)),
                            waits=wz0 if c == 0 else (), inc=s_pe if c == DC - 1 else None)
                    proj_tok[tc] = s_pe.n

                def rot(tc):
                    bkq = 2 + tc % 2
                    p_tok = proj_tok[tc]
                    w_ev = [(s_pe, p_tok)]
                    if tc == 0 and t2_guard is not None:
                        w_ev.append(t2_guard)
                    t_ev = on("scalar", lambda s, bkq=bkq, tc=tc, dstT=dstT, bias_ap=bias_ap: s.activation(
                        dstT[:, tc * 512:(tc + 1) * 512], banks[bkq][:],
                        AF.Identity, bias=bias_ap), waits=w_ev, inc=s_aa)
                    bank_free[bkq] = [(s_aa, t_ev)]
                    bkr = 4 + tc % 2
                    wz = [(s_aa, t_ev)] + bf_waits(bkr)
                    on("tensor", lambda t, bkr=bkr, tc=tc, dstT=dstT: t.matmul(
                        banks[bkr][0:64, :], permT[0:64, :],
                        dstT[0:64, tc * 512:(tc + 1) * 512],
                        start=True, stop=True, tile_position=(0, 0)),
                        waits=wz)
                    r_tok = on("tensor", lambda t, bkr=bkr, tc=tc, dstT=dstT: t.matmul(
                        banks[bkr][64:128, :], permT[64:128, :],
                        dstT[64:128, tc * 512:(tc + 1) * 512],
                        start=True, stop=True, tile_position=(64, 64)), inc=s_pe)
                    t_1 = on("vector", lambda v, bkr=bkr, tc=tc: v.tensor_mul(
                        T1[:, tc * 512:(tc + 1) * 512], banks[bkr][:],
                        sinT[:, tc * 512:(tc + 1) * 512]),
                        waits=[(s_pe, r_tok)], inc=s_vv)
                    t_2 = on("gpsimd", lambda g, tc=tc, dstT=dstT: g.tensor_mul(
                        T2[:, tc * 512:(tc + 1) * 512],
                        dstT[:, tc * 512:(tc + 1) * 512],
                        cosT[:, tc * 512:(tc + 1) * 512]),
                        waits=[(s_aa, t_ev), gn_free or (None, 0)] if tc == 0 else [(s_aa, t_ev)],
                        inc=s_gp)
                    t_c = on("vector", lambda v, tc=tc, dstT=dstT: v.tensor_add(
                        dstT[:, tc * 512:(tc + 1) * 512],
                        T1[:, tc * 512:(tc + 1) * 512],
                        T2[:, tc * 512:(tc + 1) * 512]),
                        waits=[(s_vv, t_1), (s_gp, t_2), (s_pe, r_tok)], inc=s_vv)
                    bank_free[bkr] = [(s_vv, t_1)]
                    toks[tc] = t_c

                proj(0); proj(1); rot(0); proj(2); rot(1); proj(3); rot(2); rot(3)
                return toks[3], s_pe.n

            qdone, _ = qk_rotary(0, tq, s_wq, QT, bias_q[:, pr:pr + 1],
                                 state.get("rot_t2_guard"))
            kdone, qk_mm_tok = qk_rotary(1, tk, s_wk, KT, bias_k[:, pr:pr + 1],
                                         (s_vv, qdone))
            state["rot_t2_guard"] = (s_vv, kdone)
            state["wqk_free"] = (s_pe, qk_mm_tok)

            for hl, h in ((0, h0), (1, h1)):
                wz_sg = [sg_free2] if sg_free2 is not None else []
                for tc in range(4):
                    bkg = 2 + tc % 2
                    for c in range(DC):
                        wz = []
                        if c == 0:
                            wz = [(s_wg2, tg)] + bf_waits(bkg)
                        on("tensor", lambda t, c=c, bkg=bkg, tc=tc, hl=hl: t.matmul(
                            banks[bkg][:],
                            WVG[:, 2048 + c * 256 + hl * 128: 2048 + c * 256 + hl * 128 + 128],
                            LNT[:].rearrange("p (c t) -> p c t", c=DC)[:, c, tc * 512:(tc + 1) * 512],
                            start=(c == 0), stop=(c == DC - 1)),
                            waits=wz, inc=s_pe if c == DC - 1 else None)
                    g_tok = s_pe.n
                    t_sg = on("scalar", lambda s, bkg=bkg, tc=tc, h=h: s.activation(
                        SG[:, tc * 512:(tc + 1) * 512], banks[bkg][:],
                        AF.Silu, bias=gb_f[:, h:h + 1]),
                        waits=[(s_pe, g_tok)] + wz_sg, inc=s_aa)
                    bank_free[bkg] = [(s_aa, t_sg)]
                sg_tok = s_aa.n
                last_g_mm = s_pe.n

                pp_free = [None, None]
                gnw = []
                for mb in range(4):
                    half = mb // 2
                    out_evacs = []
                    ntmax = 4 * mb + 3
                    mm1tok = {}

                    def emit_mm1(nt, mb=mb, hl=hl):
                        bks = S_BANKS[nt % 4]
                        wz = bf_waits(bks)
                        if nt == 0 and mb == 0:
                            wz = wz + [(s_vv, qdone), (s_vv, kdone),
                                       (s_vv, vev_toks[-1])]
                        mm1tok[nt] = on("tensor", lambda t, bks=bks, nt=nt, mb=mb, hl=hl: t.matmul(
                            banks[bks][:],
                            KT[hl * 64:(hl + 1) * 64, nt * 128:(nt + 1) * 128],
                            QT[hl * 64:(hl + 1) * 64, mb * 512:(mb + 1) * 512],
                            start=True, stop=True, tile_position=(hl * 64, 0)),
                            waits=wz, inc=s_pe)

                    for nt in range(min(MM1_LOOKAHEAD, ntmax + 1)):
                        emit_mm1(nt)
                    for nt in range(ntmax + 1):
                        bks = S_BANKS[nt % 4]
                        if MM1_LOOKAHEAD == 0 and nt == 0:
                            emit_mm1(0)
                        mm1 = mm1tok[nt]
                        slot = nt % 2
                        ms_d = nt - 4 * mb
                        wz_e = [(s_pe, mm1)]
                        if pp_free[slot] is not None:
                            wz_e.extend(pp_free[slot])
                        toks_e = []
                        if 0 <= ms_d <= 3:
                            toks_e.append(("vector", on("vector",
                                lambda v, bks=bks, slot=slot, ms_d=ms_d, h=h, nt=nt: v.scalar_tensor_tensor(
                                    PP[:, slot * 512 + ms_d * 128: slot * 512 + (ms_d + 1) * 128],
                                    banks[bks][:, ms_d * 128:(ms_d + 1) * 128],
                                    float(dsc[h][nt]),
                                    tril2[:, h * 128:(h + 1) * 128],
                                    ALU.mult, ALU.mult),
                                waits=wz_e, inc=s_vv)))
                        ms0 = max(0, ms_d + 1)
                        if ms0 <= 3:
                            toks_e.append(("scalar", on("scalar",
                                lambda s, bks=bks, slot=slot, ms0=ms0, h=h, nt=nt: s.activation(
                                    PP[:, slot * 512 + ms0 * 128: slot * 512 + 512],
                                    banks[bks][:, ms0 * 128:512],
                                    AF.Copy, scale=bg3[:, h * 16 + nt: h * 16 + nt + 1]),
                                waits=wz_e, inc=s_aa)))
                        bank_free[bks] = [((s_vv, t) if e == "vector" else (s_aa, t))
                                          for (e, t) in toks_e]
                        nxt = nt + (MM1_LOOKAHEAD if MM1_LOOKAHEAD else 1)
                        if nxt <= ntmax and nxt not in mm1tok:
                            emit_mm1(nxt)
                        wz_m = list(bank_free[bks])
                        first = True
                        msl = list(range(max(0, ms_d), 4))
                        for ms in msl:
                            msub = 4 * mb + ms
                            ob = OUT_BANKS[ms]
                            wzm = wz_m if first else []
                            if nt == 0 and first:
                                for b_ in OUT_BANKS:
                                    wzm = wzm + bf_waits(b_)
                            first = False
                            stop = (nt == 4 * mb + ms)
                            do_inc = stop or (ms == msl[-1])
                            last = on("tensor", lambda t, ob=ob, ms=ms, slot=slot, nt=nt, hl=hl, mb=mb, stop=stop: t.matmul(
                                banks[ob][:, 0:129],
                                PP[:, slot * 512 + ms * 128: slot * 512 + (ms + 1) * 128],
                                V3[:, hl, nt * 129:(nt + 1) * 129],
                                start=(nt == 0), stop=stop),
                                waits=wzm,
                                inc=s_pe if do_inc else None)
                            if stop:
                                out_evacs.append((ms, last))
                        pp_free[slot] = [(s_pe, s_pe.n)]
                    mtl0 = (mb % 2) * 4
                    done = {ms: t for (ms, t) in out_evacs}
                    oa_w = list(oa_free) if oa_free is not None else []
                    ev_toks = []
                    for ms in range(4):
                        ob = OUT_BANKS[ms]
                        acol = alpha[:, h * 16 + 4 * mb + ms: h * 16 + 4 * mb + ms + 1]
                        wz = [(s_pe, done[ms])] + (oa_w if ms == 0 else [])
                        if ms % 2 == 0:
                            t_o = on("vector", lambda v, ob=ob, mtl0=mtl0, ms=ms, acol=acol: v.tensor_scalar(
                                OA[:, (mtl0 + ms) * 129:(mtl0 + ms + 1) * 129],
                                banks[ob][:, 0:129], acol, None, ALU.mult),
                                waits=wz, inc=s_vv)
                            bank_free[ob] = [(s_vv, t_o)]
                            ev_toks.append(("vector", t_o))
                        else:
                            t_o = on("scalar", lambda s, ob=ob, mtl0=mtl0, ms=ms, acol=acol: s.activation(
                                OA[:, (mtl0 + ms) * 129:(mtl0 + ms + 1) * 129],
                                banks[ob][:, 0:129], AF.Copy, scale=acol),
                                waits=wz, inc=s_aa)
                            bank_free[ob] = [(s_aa, t_o)]
                            ev_toks.append(("scalar", t_o))
                    t_o1 = [t for (e, t) in ev_toks if e == "vector"][-1]
                    t_o2 = [t for (e, t) in ev_toks if e == "scalar"][-1]
                    if mb % 2 == 0:
                        continue
                    evw = [(s_vv, t_o1), (s_aa, t_o2)]
                    e1 = on("vector", lambda v: v.tensor_copy(
                        ep_rs[:, 0:8], OA3[:, :, 128:129]), waits=evw, inc=s_vv)
                    e7 = on("vector", lambda v: v.reduce_sum(
                        ep_m1[:, 0:8], OA3[:, :, 0:128], axis=AX.X), inc=s_vv)
                    e2 = on("scalar", lambda s: s.activation(
                        ep_d[:, 0:8], ep_rs[:, 0:8], AF.Abs),
                        waits=[(s_vv, e1)], inc=s_aa)
                    e4 = on("vector", lambda v: v.tensor_scalar(
                        ep_d[:, 0:8], ep_d[:, 0:8], 1.0, None, ALU.max),
                        waits=[(s_aa, e2)], inc=s_vv)
                    e5 = on("vector", lambda v: v.scalar_tensor_tensor(
                        ep_beta[:, 0:8], ep_d[:, 0:8], EPS, ep_d[:, 0:8],
                        ALU.mult, ALU.mult), waits=[(s_vv, e4)], inc=s_vv)
                    w8 = list(evw)
                    if state["junk2_guard"] is not None:
                        w8.extend(state["junk2_guard"])
                        state["junk2_guard"] = None
                    e8 = on("scalar", lambda s: s.activation(
                        junk3, OA3[:, :, 0:128], AF.Square), waits=w8, inc=s_aa)
                    state["junk2_free"] = (s_aa, e8)
                    e9 = on("vector", lambda v: v.reduce_sum(
                        ep_m2[:, 0:8], junk3, axis=AX.X), waits=[(s_aa, e8)], inc=s_vv)
                    e10 = on("vector", lambda v: v.scalar_tensor_tensor(
                        ep_rs[:, 0:8], ep_m1[:, 0:8], 1.0 / 128, ep_m1[:, 0:8],
                        ALU.mult, ALU.mult), waits=[(s_vv, e7)], inc=s_vv)
                    e11 = on("vector", lambda v: v.tensor_sub(
                        ep_sc[:, 0:8], ep_m2[:, 0:8], ep_rs[:, 0:8]),
                        waits=[(s_vv, e10), (s_vv, e9)], inc=s_vv)
                    e11b = on("vector", lambda v: v.scalar_tensor_tensor(
                        ep_bi[:, 0:8], ep_sc[:, 0:8], 1.0 / 128, ep_beta[:, 0:8],
                        ALU.mult, ALU.add), waits=[(s_vv, e11), (s_vv, e5)], inc=s_vv)
                    e12 = on("scalar", lambda s: s.activation(
                        ep_m2[:, 0:8], ep_bi[:, 0:8], AF.Ln),
                        waits=[(s_vv, e11b)], inc=s_aa)
                    e13 = on("scalar", lambda s: s.activation(
                        ep_m2[:, 0:8], ep_m2[:, 0:8], AF.Exp, scale=-0.5),
                        waits=[(s_aa, e12)], inc=s_aa)
                    e15 = on("vector", lambda v: v.scalar_tensor_tensor(
                        ep_bi[:, 0:8], ep_m1[:, 0:8], -1.0 / 128, ep_m2[:, 0:8],
                        ALU.mult, ALU.mult), waits=[(s_aa, e13)], inc=s_vv)
                    lastf = None
                    for mtl in range(8):
                        wz = [(s_vv, e15)] if mtl == 0 else []
                        if mtl == 0 and gn_free is not None:
                            wz.append(gn_free)
                        lastf = on("vector", lambda v, mtl=mtl, half=half: v.tensor_scalar(
                            T2[:, (half * 8 + mtl) * 128:(half * 8 + mtl + 1) * 128],
                            OA3[:, mtl, 0:128],
                            ep_m2[:, mtl:mtl + 1], ep_bi[:, mtl:mtl + 1],
                            ALU.mult, ALU.add), waits=wz, inc=s_vv)
                    oa_free = [(s_vv, lastf), (s_aa, e8)]
                    gnw.append(lastf)
                for grp in range(2):
                    bk = grp
                    tokp = None
                    for j in range(8):
                        mt = grp * 8 + j
                        wz = []
                        if j == 0:
                            wz = bf_waits(bk) + [(s_vv, gnw[grp])]
                            if grp == 0:
                                wz.append((s_aa, sg_tok))
                        tokp = on("tensor", lambda t, bk=bk, mt=mt: t.transpose(
                            bank_bf16(bk)[:, (mt % 8) * 128:((mt % 8) + 1) * 128],
                            T2[:, mt * 128:(mt + 1) * 128], ident[:]),
                            waits=wz, inc=s_pe if j == 7 else None)
                    wy = [(s_pe, tokp)]
                    if last_ymul is None and pr == 0:
                        wy.extend(cast_done)
                    t_y = on("vector", lambda v, bk=bk, grp=grp, h=h: v.tensor_mul(
                        yT[:, h * 2048 + grp * 1024: h * 2048 + (grp + 1) * 1024],
                        bank_bf16(bk)[:],
                        SG[:, grp * 1024:(grp + 1) * 1024]),
                        waits=wy, inc=s_vv)
                    bank_free[bk] = [(s_vv, t_y)]
                    last_ymul = t_y
                gn_free = (s_pe, s_pe.n)
                sg_free2 = (s_vv, last_ymul)
            state["wvg_free"] = (s_pe, last_g_mm)
        proj_done = (s_pe, s_pe.n)
        for oc4 in range(4):
            w = [proj_done, cast_tok[("ow", oc4 * 4)], cast_tok[("ow", oc4 * 4 + 1)],
                 cast_tok[("ow", oc4 * 4 + 2)], cast_tok[("ow", oc4 * 4 + 3)]]
            dma("sync",
                LNT[:, oc4 * 4096:(oc4 + 1) * 4096].rearrange("p (a b) -> p a b", a=4),
                s_ow_d[oc4 * 4:oc4 * 4 + 4].transpose([1, 0, 2]), s_ow, waits=w)
        ow_tok = s_ow.n
        for tt in range(NTT):
            sl = tt % 2
            w = [wait_src_store[tt]] if wait_src_store is not None else []
            if state["resid_free"][sl] is not None:
                w.append(state["resid_free"][sl])
            if tt == 0 and state["junk2_free"] is not None:
                w.append(state["junk2_free"])
                state["junk2_free"] = None
            r_tok = dma("sync", xrs_sb[:, sl * D:(sl + 1) * D],
                        src_x[tt * 128:(tt + 1) * 128, :], s_dr[sl], waits=w)
            e_tok = None
            for dc in range(2):
                bk = (2 if tt % 2 == 0 else 4) + dc
                for fc in range(NH):
                    wz = []
                    if fc == 0:
                        wz = bf_waits(bk) + [(s_ow, ow_tok)]
                        if tt == 0 and dc == 0:
                            wz.append((s_vv, last_ymul))
                    on("tensor", lambda t, fc=fc, bk=bk, tt=tt, dc=dc: t.matmul(
                        banks[bk][:],
                        yT[:, fc * 2048 + tt * 128: fc * 2048 + (tt + 1) * 128],
                        LNT[:, fc * 1024 + dc * 512: fc * 1024 + (dc + 1) * 512],
                        start=(fc == 0), stop=(fc == NH - 1)),
                        waits=wz, inc=s_pe if fc == NH - 1 else None)
                o_tok = s_pe.n
                wz = [(s_pe, o_tok)]
                if xpo_free[sl] is not None:
                    wz.append(xpo_free[sl])
                if lnv_free[sl] is not None:
                    wz.append(lnv_free[sl])
                    if dc == 1:
                        lnv_free[sl] = None
                if tt == 0 and dc == 0:
                    wz.append((s_dc2, consts2_tok))
                t_b1 = on("vector", lambda v, bk=bk, sl=sl, dc=dc: v.tensor_tensor(
                    xpo_sb[:, sl * D + dc * 512: sl * D + (dc + 1) * 512],
                    banks[bk][:], ob_rep[:, dc * 512:(dc + 1) * 512], ALU.add),
                    waits=wz, inc=s_vv)
                e_tok = on("vector", lambda v, sl=sl, dc=dc: v.tensor_add(
                    xpo_sb[:, sl * D + dc * 512: sl * D + (dc + 1) * 512],
                    xpo_sb[:, sl * D + dc * 512: sl * D + (dc + 1) * 512],
                    xrs_sb[:, sl * D + dc * 512: sl * D + (dc + 1) * 512]),
                    waits=[(s_dr[sl], r_tok), (s_vv, t_b1)], inc=s_vv)
                bank_free[bk] = [(s_vv, e_tok)]
            state["resid_free"][sl] = (s_vv, e_tok)
            st_tok = dma("sync", dst_x[tt * 128:(tt + 1) * 128, :],
                         xpo_sb[:, sl * D:(sl + 1) * D], s_st[sl],
                         waits=[(s_vv, e_tok)])
            xpo_free[sl] = (s_st[sl], st_tok)
            store_tile_tok[tt] = (s_st[sl], st_tok)
        state["yT_read_done"] = (s_pe, s_pe.n)
        state["arenaB_free"] = [(s_pe, s_pe.n), (s_vv, s_vv.n),
                                (s_aa, s_aa.n), (s_gp, s_gp.n)]
        return list(store_tile_tok)

    for b in range(B):
        A, S, O = x_in[b], x_scr[b], x_out[b]
        if N_SUBLAYERS == 1:
            emit_ffn(A, O, wait_src_store=None)
        elif N_SUBLAYERS == 2:
            tok1 = emit_ffn(A, S, wait_src_store=None)
            emit_attn(S, O, wait_src_store=tok1)
        else:
            tok1 = emit_ffn(A, S, wait_src_store=None)
            tok2 = emit_attn(S, O, wait_src_store=tok1)
            emit_ffn(O, O, wait_src_store=tok2,
                     tail_guard=state["yT_read_done"])

    P.emit("sync", lambda e, n0=s_st[0].n, n1=s_st[1].n: (
        e.wait_ge(s_st[0].h, n0), e.wait_ge(s_st[1].h, n1)))

    with nc.Block() as block:
        P.realize(block)
    stack.close()
    return nc



def kernel(**inputs):
    tabs = host_tables()
    wts = retile_weights(inputs)
    x = np.asarray(inputs["L0_attn_x"])

    nc = build_program(tabs)

    in_maps = []
    for c in range(NCORES):
        in_maps.append({
            "x_in": np.ascontiguousarray(x[c * B:(c + 1) * B]),
            "wgu_t": wts["wgu_t"], "wd_t": wts["wd_t"],
            "qw_t": wts["qw_t"], "kw_t": wts["kw_t"],
            "vw_t": wts["vw_t"], "gw_t": wts["gw_t"], "ow_t": wts["ow_t"],
            "q_b": wts["q_b"], "k_b": wts["k_b"], "v_b": wts["v_b"],
            "g_b": wts["g_b"], "o_b": wts["o_b"],
            "c_sinT": tabs["sinT"], "c_cosT": tabs["cosT"],
            "c_permT": tabs["permT"], "c_ident": tabs["ident"],
            "c_identf": tabs["identf"],
            "c_bg3": tabs["bg3"], "c_tril2": tabs["tril2"],
            "c_alpha": tabs["alpha"],
        })

    res = run_bass_kernel_spmd(nc, in_maps, core_ids=list(range(NCORES)))
    return np.concatenate([r["x_out"] for r in res.results], axis=0)


# version 07
# speedup vs baseline: 1.0323x; 1.0323x over previous
import contextlib
import numpy as np
import ml_dtypes
import concourse.bass as bass
import concourse.mybir as mybir
from concourse.bass_utils import run_bass_kernel_spmd

F32 = mybir.dt.float32
BF16 = mybir.dt.bfloat16
I32 = mybir.dt.int32
AF = mybir.ActivationFunctionType
ALU = mybir.AluOpType
AX = mybir.AxisListType
BF = ml_dtypes.bfloat16

NCORES = 8
B = 4
T = 2048
D = 1024
HID = 3072
NHC = HID // 128
DC = D // 128
NTT = T // 128
NH = 16
EPS = 1e-5
QDEQ = 1.0 / 127.0
DOWN_SCALE = 0.5 * QDEQ * QDEQ

N_SUBLAYERS = 3
MM1_LOOKAHEAD = 3



def host_tables():
    t = {}
    half = 1.0 / np.power(10000.0, np.linspace(0.0, 1.0, 32))
    angle = np.repeat(half, 2)
    idx = np.arange(T, dtype=np.float64)
    sinT = np.sin(idx[:, None] * angle[None, :]).T.astype(np.float32)
    cosT = np.cos(idx[:, None] * angle[None, :]).T.astype(np.float32)
    t["sinT"] = np.ascontiguousarray(np.concatenate([sinT, sinT], 0)).astype(BF)
    t["cosT"] = np.ascontiguousarray(np.concatenate([cosT, cosT], 0)).astype(BF)
    s64 = np.zeros((64, 64), dtype=np.float32)
    for i in range(32):
        s64[2 * i + 1, 2 * i] = -1.0
        s64[2 * i, 2 * i + 1] = 1.0
    t["permT"] = np.ascontiguousarray(np.concatenate([s64, s64], 0)).astype(BF)
    t["ident"] = np.eye(128, dtype=np.float32).astype(BF)
    t["identf"] = np.eye(128, dtype=np.float32)
    dec = np.log(1.0 - 2.0 ** (-5.0 - np.arange(NH, dtype=np.float64)))
    n_ = np.arange(128, dtype=np.float64)
    bg3 = 0.125 * np.exp(-dec[None, :, None]
                         * (128.0 * np.arange(16)[None, None, :] + n_[:, None, None]))
    t["bg3"] = np.ascontiguousarray(bg3.reshape(128, NH * 16)).astype(np.float32)
    m_ = np.arange(128)
    trilmask = (m_[None, :] >= n_[:, None]).astype(np.float64)
    tril2 = np.stack([trilmask * 0.125 * np.exp(-dec[h] * n_)[:, None]
                      for h in range(NH)])
    t["tril2"] = tril2.astype(np.float32)
    mg = np.arange(T, dtype=np.float64)
    alpha = np.zeros((128, NH * 16), dtype=np.float64)
    for h in range(NH):
        csum = np.cumsum(np.exp(dec[h] * mg))
        am = np.exp(dec[h] * mg) / np.sqrt(csum)
        alpha[:, h * 16:(h + 1) * 16] = am.reshape(16, 128).T
    t["alpha"] = alpha.astype(np.float32)
    t["diag_scale"] = np.exp(-dec[:, None] * 128.0 * np.arange(16)[None, :])
    return t


def retile_weights(ins):
    out = {}
    wg = np.asarray(ins["L0_ffn_w_gate"])
    wu = np.asarray(ins["L0_ffn_w_up"])
    wd = np.asarray(ins["L0_ffn_w_down"])

    def tile_gate(w):
        return np.ascontiguousarray(w.reshape(NHC, 128, DC, 128).transpose(0, 3, 2, 1))

    out["wgu_t"] = np.ascontiguousarray(
        np.stack([tile_gate(wg), tile_gate(wu)], axis=1))
    out["wd_t"] = np.ascontiguousarray(
        wd.reshape(D, NHC, 128).transpose(1, 2, 0))

    def tile_qk(w):
        return np.ascontiguousarray(
            w.reshape(DC, 128, NH // 2, 128).transpose(2, 1, 0, 3))

    out["qw_t"] = tile_qk(np.asarray(ins["L0_attn_q_w"]))
    out["kw_t"] = tile_qk(np.asarray(ins["L0_attn_k_w"]))

    def tile_vg(w):
        return np.ascontiguousarray(
            w.reshape(DC, 128, NH // 2, 256).transpose(2, 1, 0, 3))

    out["vw_t"] = tile_vg(np.asarray(ins["L0_attn_v_w"]))
    out["gw_t"] = tile_vg(np.asarray(ins["L0_attn_g_w"]))
    out["ow_t"] = np.ascontiguousarray(np.asarray(ins["L0_attn_o_w"])).reshape(NH, 128, D)
    for k in ("q_b", "k_b", "v_b", "g_b", "o_b"):
        out[k] = np.ascontiguousarray(np.asarray(ins["L0_attn_" + k]))
    return out



class Sem:
    def __init__(self, stack, nc, name):
        self.h = stack.enter_context(nc.semaphore(name))
        self.n = 0

    def bump(self, k=1):
        self.n += k
        return self.n


class Prog:
    ENGINES = ("sync", "tensor", "vector", "scalar", "gpsimd")

    def __init__(self):
        self.streams = {e: [] for e in self.ENGINES}

    def emit(self, eng, fn):
        self.streams[eng].append(fn)

    def realize(self, block):
        for name in self.ENGINES:
            fns = self.streams[name]
            getattr(block, name)(lambda eng, fns=fns: [f(eng) for f in fns])


def build_program(tabs):
    nc = bass.Bass("TRN2", target_bir_lowering=False)
    stack = contextlib.ExitStack()

    x_in = nc.dram_tensor("x_in", [B, T, D], F32, kind="ExternalInput").ap()
    x_out = nc.dram_tensor("x_out", [B, T, D], F32, kind="ExternalOutput").ap()
    x_scr = nc.dram_tensor("x_scr", [B, T, D], F32, kind="Internal").ap()

    wgu_t = nc.dram_tensor("wgu_t", [NHC, 2, 128, DC, 128], I32, kind="ExternalInput").ap()
    wd_t = nc.dram_tensor("wd_t", [NHC, 128, D], I32, kind="ExternalInput").ap()
    qw_t = nc.dram_tensor("qw_t", [NH // 2, 128, DC, 128], F32, kind="ExternalInput").ap()
    kw_t = nc.dram_tensor("kw_t", [NH // 2, 128, DC, 128], F32, kind="ExternalInput").ap()
    vw_t = nc.dram_tensor("vw_t", [NH // 2, 128, DC, 256], F32, kind="ExternalInput").ap()
    gw_t = nc.dram_tensor("gw_t", [NH // 2, 128, DC, 256], F32, kind="ExternalInput").ap()
    ow_t = nc.dram_tensor("ow_t", [NH, 128, D], F32, kind="ExternalInput").ap()
    q_b = nc.dram_tensor("q_b", [D], F32, kind="ExternalInput").ap()
    k_b = nc.dram_tensor("k_b", [D], F32, kind="ExternalInput").ap()
    v_b = nc.dram_tensor("v_b", [2 * D], F32, kind="ExternalInput").ap()
    g_b = nc.dram_tensor("g_b", [2 * D], F32, kind="ExternalInput").ap()
    o_b = nc.dram_tensor("o_b", [D], F32, kind="ExternalInput").ap()

    c_sinT = nc.dram_tensor("c_sinT", [128, T], BF16, kind="ExternalInput").ap()
    c_cosT = nc.dram_tensor("c_cosT", [128, T], BF16, kind="ExternalInput").ap()
    c_permT = nc.dram_tensor("c_permT", [128, 64], BF16, kind="ExternalInput").ap()
    c_ident = nc.dram_tensor("c_ident", [128, 128], BF16, kind="ExternalInput").ap()
    c_identf = nc.dram_tensor("c_identf", [128, 128], F32, kind="ExternalInput").ap()
    c_bg3 = nc.dram_tensor("c_bg3", [128, NH * 16], F32, kind="ExternalInput").ap()
    c_tril2 = nc.dram_tensor("c_tril2", [NH, 128, 128], F32, kind="ExternalInput").ap()
    c_alpha = nc.dram_tensor("c_alpha", [128, NH * 16], F32, kind="ExternalInput").ap()

    s_wgu = nc.dram_tensor("s_wgu", [NHC, 2, 128, DC * 128], BF16, kind="Internal").ap()
    s_wd = nc.dram_tensor("s_wd", [NHC, 128, D], BF16, kind="Internal").ap()
    s_qw = nc.dram_tensor("s_qw", [NH // 2, 128, DC * 128], BF16, kind="Internal").ap()
    s_kw = nc.dram_tensor("s_kw", [NH // 2, 128, DC * 128], BF16, kind="Internal").ap()
    s_vw = nc.dram_tensor("s_vw", [NH // 2, 128, DC * 256], BF16, kind="Internal").ap()
    s_gw = nc.dram_tensor("s_gw", [NH // 2, 128, DC * 256], BF16, kind="Internal").ap()
    s_ow_d = nc.dram_tensor("s_ow", [NH, 128, D], BF16, kind="Internal").ap()

    def sb(name, shape, dt):
        return stack.enter_context(nc.sbuf_tensor(name, shape, dt))

    LNT = sb("LNT", [128, DC * T], BF16)
    arenaA = sb("arenaA", [128, 16 * T], BF16)
    arenaB = sb("arenaB", [128, NHC * D], BF16)
    wgu_sb = sb("wgu_sb", [128, 2 * 2048], BF16)
    sg_sb = sb("sg_sb", [128, 1024], BF16)
    xin_sb = sb("xin_sb", [128, 2 * D], F32)
    xrs_sb = sb("xrs_sb", [128, 2 * D], F32)
    xpo_sb = sb("xpo_sb", [128, 2 * D], F32)
    st_s1 = sb("st_s1", [128, 16], F32)
    st_s2 = sb("st_s2", [128, 16], F32)
    st_u = sb("st_u", [128, 16], F32)
    st_v = sb("st_v", [128, 16], F32)
    st_r = sb("st_r", [128, 16], F32)
    st_nb = sb("st_nb", [128, 16], F32)
    sinT = sb("sinT", [128, T], BF16)
    cosT = sb("cosT", [128, T], BF16)
    permT = sb("permT", [128, 64], BF16)
    ident = sb("ident", [128, 128], BF16)
    identf = sb("identf", [128, 128], F32)
    bg3 = sb("bg3", [128, NH * 16], F32)
    tril2 = sb("tril2", [128, NH * 128], F32)
    alpha = sb("alpha", [128, NH * 16], F32)
    bias_q = sb("bias_q", [128, DC], F32)
    bias_k = sb("bias_k", [128, DC], F32)
    gb_f = sb("gb_f", [128, NH], F32)
    vb_rep = sb("vb_rep", [128, 2 * D], BF16)
    ob_rep = sb("ob_rep", [128, D], BF16)
    bstage = sb("bstage", [128, 128], F32)
    ep_rs = sb("ep_rs", [128, 16], F32)
    ep_d = sb("ep_d", [128, 16], F32)
    ep_beta = sb("ep_beta", [128, 16], F32)
    ep_m1 = sb("ep_m1", [128, 16], F32)
    ep_m2 = sb("ep_m2", [128, 16], F32)
    ep_sc = sb("ep_sc", [128, 16], F32)
    ep_bi = sb("ep_bi", [128, 16], F32)
    c_eps = sb("c_eps", [128, 1], F32)

    junk_sb2 = [arenaA[:, 28672:30720].bitcast(F32)[:, 0:D],
                arenaA[:, 30720:32768].bitcast(F32)[:, 0:D]]

    banks = [stack.enter_context(nc.psum_tensor(f"pb{i}", [128, 512], F32))
             for i in range(8)]

    def bank_bf16(i):
        return banks[i][:].bitcast(BF16)

    s_dc = Sem(stack, nc, "s_dc")
    s_dc2 = Sem(stack, nc, "s_dc2")
    s_bs = Sem(stack, nc, "s_bs")
    s_cl = [Sem(stack, nc, f"s_cl{i}") for i in range(2)]
    s_cs = [Sem(stack, nc, f"s_cs{i}") for i in range(2)]
    s_wg = [Sem(stack, nc, f"s_wg{i}") for i in range(2)]
    s_wdT = Sem(stack, nc, "s_wdT")
    s_dx = [Sem(stack, nc, f"s_dx{i}") for i in range(2)]
    s_dr = [Sem(stack, nc, f"s_dr{i}") for i in range(2)]
    s_st = [Sem(stack, nc, f"s_st{i}") for i in range(2)]
    s_wq = Sem(stack, nc, "s_wq")
    s_wk = Sem(stack, nc, "s_wk")
    s_wv = Sem(stack, nc, "s_wv")
    s_wg2 = Sem(stack, nc, "s_wg2")
    s_ow = Sem(stack, nc, "s_ow")
    s_pe = Sem(stack, nc, "s_pe")
    s_vv = Sem(stack, nc, "s_vv")
    s_aa = Sem(stack, nc, "s_aa")
    s_gp = Sem(stack, nc, "s_gp")

    P = Prog()

    def on(eng, fn, waits=(), inc=None, inc_n=1):
        waits = tuple((s, v) for (s, v) in waits if s is not None)
        tok = inc.bump(inc_n) if inc is not None else None

        def f(e, fn=fn, waits=waits, inc=inc, inc_n=inc_n):
            for (s, v) in waits:
                e.wait_ge(s.h, v)
            r = fn(e)
            if inc is not None:
                r.then_inc(inc.h, inc_n)
        P.emit(eng, f)
        return tok

    def dma(queue, dst, src, sem, waits=()):
        waits = tuple((s, v) for (s, v) in waits if s is not None)
        tok = sem.bump(16)

        def f(e, dst=dst, src=src, sem=sem, waits=waits):
            for (s, v) in waits:
                e.wait_ge(s.h, v)
            e.dma_start(out=dst, in_=src).then_inc(sem.h, 16)
        P.emit(queue, f)
        return tok

    bank_free = [None] * 8

    def bf_waits(i):
        return list(bank_free[i]) if bank_free[i] is not None else []

    dma("sync", sinT[:], c_sinT, s_dc)
    dma("sync", cosT[:], c_cosT, s_dc)
    dma("sync", permT[:], c_permT, s_dc)
    dma("sync", ident[:], c_ident, s_dc)
    dma("sync", identf[:], c_identf, s_dc)
    dma("sync", bg3[:], c_bg3, s_dc)
    dma("sync", alpha[:], c_alpha, s_dc)
    for h0 in range(0, NH, 4):
        dma("sync", tril2[:].rearrange("p (h m) -> p h m", h=NH)[:, h0:h0 + 4, :],
            c_tril2[h0:h0 + 4].transpose([1, 0, 2]), s_dc)
    consts_tok = s_dc.n
    dma("gpsimd", vb_rep[:], v_b[None, :].broadcast_to([128, 2 * D]), s_dc2)
    dma("gpsimd", ob_rep[:], o_b[None, :].broadcast_to([128, D]), s_dc2)
    consts2_tok = s_dc2.n

    t_eps = on("vector", lambda v: v.memset(c_eps[:], EPS), inc=s_vv)
    on("scalar", lambda s: s.wait_ge(s_vv.h, t_eps))
    t_z = on("vector", lambda v: v.memset(bstage[:], 0.0), inc=s_vv)
    small_b = [(q_b, DC, bias_q), (k_b, DC, bias_k), (g_b, NH, gb_f)]
    for i, (src, rows, dst) in enumerate(small_b):
        t_b = dma("sync", bstage[0:rows, :], src.rearrange("(c p) -> c p", p=128),
                  s_bs, waits=[(s_vv, s_vv.n)])
        t_t = on("tensor", lambda t: t.transpose(banks[0][:, 0:128], bstage[:], identf[:]),
                 waits=[(s_bs, t_b), (s_dc, consts_tok)] + bf_waits(0), inc=s_pe)
        t_c = on("vector", lambda v, dst=dst, rows=rows: v.tensor_copy(
            dst[:], banks[0][:, 0:rows]), waits=[(s_pe, t_t)], inc=s_vv)
        bank_free[0] = [(s_vv, t_c)]
        if i < len(small_b) - 1:
            on("vector", lambda v, rows=rows: v.memset(bstage[0:rows, :], 0.0), inc=s_vv)
    bias_done = s_vv.n

    cast_jobs = []
    for hc in range(NHC):
        cast_jobs.append(("wgu", hc, wgu_t[hc].rearrange("g p c j -> p g (c j)"),
                          s_wgu[hc].rearrange("g p f -> p g f"), 2048))
    for hc in range(NHC):
        cast_jobs.append(("wd", hc, wd_t[hc], s_wd[hc], 1024))
    for pr in range(NH // 2):
        cast_jobs.append(("qw", pr, qw_t[pr].rearrange("p c j -> p (c j)"), s_qw[pr], 1024))
        cast_jobs.append(("kw", pr, kw_t[pr].rearrange("p c j -> p (c j)"), s_kw[pr], 1024))
    for pr in range(NH // 2):
        cast_jobs.append(("vw", pr, vw_t[pr].rearrange("p c j -> p (c j)"), s_vw[pr], 2048))
        cast_jobs.append(("gw", pr, gw_t[pr].rearrange("p c j -> p (c j)"), s_gw[pr], 2048))
    for h in range(NH):
        cast_jobs.append(("ow", h, ow_t[h], s_ow_d[h], 1024))

    cast_tok = {}
    store_toks = []
    for i, (kind, idx, src, dst, width) in enumerate(cast_jobs):
        slot = i % 2
        sl = arenaA[:, 24576 + slot * 2048: 24576 + slot * 2048 + width]
        if kind == "wgu":
            sl = sl.rearrange("p (g f) -> p g f", g=2)
        w = []
        if i >= 2:
            w.append((s_cs[slot], store_toks[i - 2]))
        g_tok = dma("gpsimd", sl, src, s_cl[slot], waits=w)
        st = dma("sync", dst, sl, s_cs[slot], waits=[(s_cl[slot], g_tok)])
        store_toks.append(st)
        cast_tok[(kind, idx)] = (s_cs[slot], st)
    cast_done = [(s_cs[0], s_cs[0].n), (s_cs[1], s_cs[1].n)]

    apply_toks = [None, None]
    xpo_free = [None, None]
    ln_tile_tok = [0] * NTT
    store_tile_tok = [None] * NTT
    lnv_free = [None, None]
    sg_free = [None, None]

    def emit_ln(src_x, wait_store_tok=None, tail_guard=None):
        GS = 4
        rd_toks = {}
        grp_math = {}
        slot_prev = state["xin_slot_prev"]

        def pass_a(g):
            for tt in range(g * GS, (g + 1) * GS):
                sl = tt % 2
                w = [wait_store_tok[tt]] if wait_store_tok is not None else []
                w.extend(slot_prev[sl])
                xt_tok = dma("sync", xin_sb[:, sl * D:(sl + 1) * D],
                             src_x[tt * 128:(tt + 1) * 128, :], s_dx[sl], waits=w)
                xin = xin_sb[:, sl * D:(sl + 1) * D]
                t_r = on("vector", lambda v, xin=xin, tt=tt: v.reduce_sum(
                    st_s1[:, tt:tt + 1], xin, axis=AX.X),
                    waits=[(s_dx[sl], xt_tok)], inc=s_vv)
                w_sq = [(s_dx[sl], xt_tok)]
                if tt == 0 and tail_guard is not None:
                    w_sq.append(tail_guard)
                t_q = on("scalar", lambda s, xin=xin, tt=tt: s.activation(
                    junk_sb2[tt % 2], xin, AF.Square, accum_out=st_s2[:, tt:tt + 1]),
                    waits=w_sq, inc=s_aa)
                rd_toks[tt] = [(s_vv, t_r), (s_aa, t_q)]
                slot_prev[sl] = [(s_vv, t_r), (s_aa, t_q)]

        def pass_b(g):
            sl4 = slice(g * GS, (g + 1) * GS)
            last_red = rd_toks[(g + 1) * GS - 1][0][1]
            last_sq = rd_toks[(g + 1) * GS - 1][1][1]
            t_u = on("vector", lambda v: v.scalar_tensor_tensor(
                st_u[:, sl4], st_s1[:, sl4], 1.0 / D, st_s1[:, sl4],
                ALU.mult, ALU.mult), waits=[(s_vv, last_red)], inc=s_vv)
            t_vd = on("vector", lambda v: v.tensor_sub(
                st_v[:, sl4], st_s2[:, sl4], st_u[:, sl4]),
                waits=[(s_vv, t_u), (s_aa, last_sq)], inc=s_vv)
            t_l = on("scalar", lambda s: s.activation(
                st_r[:, sl4], st_v[:, sl4], AF.Ln, scale=1.0 / D, bias=c_eps[:, 0:1]),
                waits=[(s_vv, t_vd)], inc=s_aa)
            t_e = on("scalar", lambda s: s.activation(
                st_r[:, sl4], st_r[:, sl4], AF.Exp, scale=-0.5),
                waits=[(s_aa, t_l)], inc=s_aa)
            t_nb = on("vector", lambda v: v.scalar_tensor_tensor(
                st_nb[:, sl4], st_s1[:, sl4], -1.0 / D, st_r[:, sl4],
                ALU.mult, ALU.mult), waits=[(s_aa, t_e)], inc=s_vv)
            grp_math[g] = t_nb

        def pass_c(g):
            t_nb = grp_math[g]
            for tt in range(g * GS, (g + 1) * GS):
                sl = tt % 2
                w = list(slot_prev[sl])
                if wait_store_tok is not None:
                    w.append(wait_store_tok[tt])
                xt_tok = dma("sync", xin_sb[:, sl * D:(sl + 1) * D],
                             src_x[tt * 128:(tt + 1) * 128, :], s_dx[sl], waits=w)
                xin = xin_sb[:, sl * D:(sl + 1) * D]
                lnv = xpo_sb[:, sl * D:(sl + 1) * D].bitcast(BF16)[:, 0:D]
                w_ap = [(s_vv, t_nb), (s_dx[sl], xt_tok)]
                if xpo_free[sl] is not None:
                    w_ap.append(xpo_free[sl])
                if lnv_free[sl] is not None:
                    w_ap.append(lnv_free[sl])
                t_ap = on("vector", lambda v, xin=xin, lnv=lnv, tt=tt: v.tensor_scalar(
                    lnv, xin, st_r[:, tt:tt + 1], st_nb[:, tt:tt + 1],
                    ALU.mult, ALU.add), waits=w_ap, inc=s_vv)
                apply_toks[sl] = (s_vv, t_ap)
                slot_prev[sl] = [(s_vv, t_ap)]
                bk = 6 + (tt % 2)
                tokp = None
                for c in range(DC):
                    w2 = []
                    if c == 0:
                        w2 = [(s_vv, t_ap)] + bf_waits(bk)
                    tokp = on("tensor", lambda t, bk=bk, c=c, lnv=lnv: t.transpose(
                        bank_bf16(bk)[:, c * 128:(c + 1) * 128],
                        lnv[:, c * 128:(c + 1) * 128], ident[:]),
                        waits=w2, inc=s_pe if c == DC - 1 else None)
                lnv_free[sl] = (s_pe, tokp)
                dst = LNT[:].rearrange("p (c t) -> p c t", c=DC)[:, :, tt * 128:(tt + 1) * 128]
                t_ev = on("vector", lambda v, bk=bk, dst=dst: v.tensor_copy(
                    dst, bank_bf16(bk)[:].rearrange("p (c t) -> p c t", c=DC)),
                    waits=[(s_pe, tokp)], inc=s_vv)
                bank_free[bk] = [(s_vv, t_ev)]
                ln_tile_tok[tt] = t_ev
        pass_a(0)
        pass_a(1)
        pass_b(0)
        pass_c(0)
        pass_a(2)
        pass_b(1)
        pass_c(1)
        pass_a(3)
        pass_b(2)
        pass_c(2)
        pass_b(3)
        pass_c(3)
        return ln_tile_tok[NTT - 1]

    H_view = arenaA
    wdT_sb = arenaB
    state = {"wgu_slot_free": [None, None], "arenaB_free": None,
             "resid_free": [None, None], "yT_read_done": None,
             "wqk_free": None, "wvg_free": None,
             "junk2_free": None, "junk2_guard": None,
             "xin_slot_prev": [[], []]}

    def emit_ffn(src_x, dst_x, wait_src_store=None, tail_guard=None):
        def load_wgu(hc):
            slot = hc % 2
            w = [cast_tok[("wgu", hc)]]
            if state["wgu_slot_free"][slot] is not None:
                w.append((s_pe, state["wgu_slot_free"][slot]))
            return dma("sync",
                       wgu_sb[:, slot * 2048:(slot + 1) * 2048].rearrange(
                           "p (g f) -> p g f", g=2),
                       s_wgu[hc].rearrange("g p f -> p g f"), s_wg[slot], waits=w)

        pre0 = {0: load_wgu(0), 1: load_wgu(1)}
        wd_tok = None
        for hc in range(NHC):
            w = [cast_tok[("wd", hc)]]
            if hc == 0 and state["arenaB_free"] is not None:
                w.extend(state["arenaB_free"])
            wd_tok = dma("sync", wdT_sb[:, hc * 1024:(hc + 1) * 1024],
                         s_wd[hc], s_wdT, waits=w)
        ln_tok = emit_ln(src_x, wait_store_tok=wait_src_store, tail_guard=tail_guard)
        h_done_halves = []
        for half in range(2):
            t0 = half * 1024
            pre = pre0 if half == 0 else {0: load_wgu(0), 1: load_wgu(1)}
            for hc in range(NHC):
                slot = hc % 2
                wtok = pre[hc] if hc in pre else load_wgu(hc)
                swg = s_wg[slot]
                for tc in range(2):
                    gbk, ubk = 2 * tc, 2 * tc + 1
                    rhs = LNT[:].rearrange("p (c t) -> p c t", c=DC)[
                        :, :, t0 + tc * 512: t0 + (tc + 1) * 512]
                    for c in range(DC):
                        wz = []
                        if c == 0:
                            wz = [(swg, wtok)] + bf_waits(gbk)
                            if hc == 0:
                                wz.append((s_vv, ln_tile_tok[(t0 + tc * 512 + 511) // 128]))
                        on("tensor", lambda t, c=c, gbk=gbk, rhs=rhs, slot=slot: t.matmul(
                            banks[gbk][:],
                            wgu_sb[:, slot * 2048 + c * 128: slot * 2048 + (c + 1) * 128],
                            rhs[:, c, :], start=(c == 0), stop=(c == DC - 1)),
                            waits=wz, inc=s_pe if c == DC - 1 else None)
                    g_tok = s_pe.n
                    for c in range(DC):
                        wz = []
                        if c == 0:
                            wz = bf_waits(ubk)
                        on("tensor", lambda t, c=c, ubk=ubk, rhs=rhs, slot=slot: t.matmul(
                            banks[ubk][:],
                            wgu_sb[:, slot * 2048 + 1024 + c * 128: slot * 2048 + 1024 + (c + 1) * 128],
                            rhs[:, c, :], start=(c == 0), stop=(c == DC - 1)),
                            waits=wz, inc=s_pe if c == DC - 1 else None)
                    u_tok = s_pe.n
                    sgv = sg_sb[:, tc * 512:(tc + 1) * 512]
                    w_sg = [(s_pe, g_tok)]
                    if sg_free[tc] is not None:
                        w_sg.append(sg_free[tc])
                    a_tok = on("scalar", lambda s, gbk=gbk, sgv=sgv: s.activation(
                        sgv, banks[gbk][:], AF.Silu, scale=QDEQ),
                        waits=w_sg, inc=s_aa)
                    bank_free[gbk] = [(s_aa, a_tok)]
                    hdst = H_view[:, hc * 1024 + tc * 512: hc * 1024 + (tc + 1) * 512]
                    v_tok = on("vector", lambda v, sgv=sgv, ubk=ubk, hdst=hdst: v.tensor_tensor(
                        hdst, sgv, banks[ubk][:], ALU.mult),
                        waits=[(s_pe, u_tok), (s_aa, a_tok)], inc=s_vv)
                    bank_free[ubk] = [(s_vv, v_tok)]
                    sg_free[tc] = (s_vv, v_tok)
                state["wgu_slot_free"][slot] = s_pe.n
            h_done = s_vv.n
            for ttl in range(8):
                tt = half * 8 + ttl
                sl = tt % 2
                w = [wait_src_store[tt]] if wait_src_store is not None else []
                if state["resid_free"][sl] is not None:
                    w.append(state["resid_free"][sl])
                r_tok = dma("sync", xrs_sb[:, sl * D:(sl + 1) * D],
                            src_x[tt * 128:(tt + 1) * 128, :], s_dr[sl], waits=w)
                e_tok = None
                for dc in range(2):
                    bk = (4 if ttl % 2 == 0 else 6) + dc
                    for hc in range(NHC):
                        wz = []
                        if hc == 0:
                            wz = [(s_wdT, wd_tok), (s_vv, h_done)] + bf_waits(bk)
                        on("tensor", lambda t, hc=hc, bk=bk, ttl=ttl, dc=dc: t.matmul(
                            banks[bk][:],
                            H_view[:, hc * 1024 + ttl * 128: hc * 1024 + (ttl + 1) * 128],
                            wdT_sb[:, hc * 1024 + dc * 512: hc * 1024 + (dc + 1) * 512],
                            start=(hc == 0), stop=(hc == NHC - 1)),
                            waits=wz, inc=s_pe if hc == NHC - 1 else None)
                    d_tok = s_pe.n
                    wz = [(s_pe, d_tok), (s_dr[sl], r_tok)]
                    if xpo_free[sl] is not None:
                        wz.append(xpo_free[sl])
                    if lnv_free[sl] is not None:
                        wz.append(lnv_free[sl])
                        if dc == 1:
                            lnv_free[sl] = None
                    e_tok = on("vector", lambda v, bk=bk, sl=sl, dc=dc: v.scalar_tensor_tensor(
                        xpo_sb[:, sl * D + dc * 512: sl * D + (dc + 1) * 512],
                        banks[bk][:], DOWN_SCALE,
                        xrs_sb[:, sl * D + dc * 512: sl * D + (dc + 1) * 512],
                        ALU.mult, ALU.add), waits=wz, inc=s_vv)
                    bank_free[bk] = [(s_vv, e_tok)]
                state["resid_free"][sl] = (s_vv, e_tok)
                st_tok = dma("sync", dst_x[tt * 128:(tt + 1) * 128, :],
                             xpo_sb[:, sl * D:(sl + 1) * D], s_st[sl],
                             waits=[(s_vv, e_tok)])
                xpo_free[sl] = (s_st[sl], st_tok)
                store_tile_tok[tt] = (s_st[sl], st_tok)
            h_done_halves.append(h_done)
        state["arenaB_free"] = [(s_pe, s_pe.n)]
        return list(store_tile_tok)


    OUT_BANKS = (6, 7, 2, 3)
    S_BANKS = (4, 5, 0, 1)
    OFF_V, OFF_SG, OFF_QT, OFF_KT = 0, 4128, 6176, 8224
    OFF_T1, OFF_T2, OFF_PP, OFF_OA = 10272, 12320, 14368, 15392
    OFF_WQK, OFF_WVG = 17456, 19504
    dsc = tabs["diag_scale"]

    def emit_attn(src_x, dst_x, wait_src_store):
        yT = arenaA
        VALL = arenaB[:, OFF_V:OFF_V + 4128]
        V3 = VALL.rearrange("p (hl z) -> p hl z", hl=2)
        SG = arenaB[:, OFF_SG:OFF_SG + 2048]
        QT = arenaB[:, OFF_QT:OFF_QT + 2048]
        KT = arenaB[:, OFF_KT:OFF_KT + 2048]
        T1 = arenaB[:, OFF_T1:OFF_T1 + 2048]
        T2 = arenaB[:, OFF_T2:OFF_T2 + 2048]
        PP = arenaB[:, OFF_PP:OFF_PP + 1024]
        OA = arenaB[:, OFF_OA:OFF_OA + 2064].bitcast(F32)
        OA3 = OA.rearrange("p (a b) -> p a b", b=129)
        WQK = arenaB[:, OFF_WQK:OFF_WQK + 2048]
        WVG = arenaB[:, OFF_WVG:OFF_WVG + 4096]
        junk3 = xrs_sb[:, 0:D].rearrange("p (a b) -> p a b", a=8)

        state["junk2_guard"] = [t for t in state["resid_free"] if t is not None]
        ln_tok = emit_ln(src_x, wait_store_tok=wait_src_store,
                         tail_guard=state["yT_read_done"])
        abf = state["arenaB_free"] or []
        t_ones = on("vector", lambda v: v.memset(VALL, 1.0),
                    waits=abf + [(s_dc2, consts2_tok)], inc=s_vv)

        oa_free = None
        gn_free = None
        sg_free2 = None
        last_mm2 = None
        last_ymul = None
        for pr in range(NH // 2):
            h0, h1 = 2 * pr, 2 * pr + 1
            w = [cast_tok[("qw", pr)]]
            if state["wqk_free"] is not None:
                w.append(state["wqk_free"])
            elif abf:
                w.extend(abf)
            tq = dma("sync", WQK[:, 0:1024], s_qw[pr], s_wq, waits=w)
            w2k = [cast_tok[("kw", pr)]]
            if state["wqk_free"] is not None:
                w2k.append(state["wqk_free"])
            tk = dma("sync", WQK[:, 1024:2048], s_kw[pr], s_wk, waits=w2k)
            w = [cast_tok[("vw", pr)]]
            if state["wvg_free"] is not None:
                w.append(state["wvg_free"])
            elif abf:
                w.extend(abf)
            tv = dma("sync", WVG[:, 0:2048], s_vw[pr], s_wv, waits=w)
            w2g = [cast_tok[("gw", pr)]]
            if state["wvg_free"] is not None:
                w2g.append(state["wvg_free"])
            tg = dma("sync", WVG[:, 2048:4096], s_gw[pr], s_wg2, waits=w2g)

            vev_toks = []
            for tt in range(NTT):
                bkv = tt % 2
                for c in range(DC):
                    wz = []
                    if c == 0:
                        wz = [(s_wv, tv)] + bf_waits(bkv)
                        if pr == 0 and tt < 2:
                            wz.append((s_vv, ln_tok))
                    on("tensor", lambda t, c=c, bkv=bkv, tt=tt: t.matmul(
                        banks[bkv][:, 0:256],
                        LNT[:, c * 2048 + tt * 128: c * 2048 + (tt + 1) * 128],
                        WVG[:, c * 256:(c + 1) * 256],
                        start=(c == 0), stop=(c == DC - 1)),
                        waits=wz, inc=s_pe if c == DC - 1 else None)
                v_tok = s_pe.n
                wz = [(s_pe, v_tok), (s_vv, t_ones)]
                t_ve = on("vector", lambda v, bkv=bkv, tt=tt, pr=pr: v.tensor_tensor(
                    V3[:, :, tt * 129: tt * 129 + 128],
                    banks[bkv][:, 0:256].rearrange("p (hl j) -> p hl j", hl=2),
                    vb_rep[:, pr * 256:(pr + 1) * 256].rearrange("p (hl j) -> p hl j", hl=2),
                    ALU.add), waits=wz, inc=s_vv)
                bank_free[bkv] = [(s_vv, t_ve)]
                vev_toks.append(t_ve)

            def qk_rotary(which, wtok, wsem, dstT, bias_ap, t2_guard=None):
                toks = {}
                proj_tok = {}

                def proj(tc):
                    bkq = 2 + tc % 2
                    wz0 = [(wsem, wtok)] + bf_waits(bkq)
                    if pr == 0 and tc < 2 and which == 0:
                        wz0.append((s_vv, ln_tok))
                    if pr == 0 and tc == 0 and which == 0:
                        wz0.append((s_dc, consts_tok))
                    for c in range(DC):
                        on("tensor", lambda t, c=c, bkq=bkq, tc=tc, which=which: t.matmul(
                            banks[bkq][:],
                            WQK[:, which * 1024 + c * 128: which * 1024 + (c + 1) * 128],
                            LNT[:].rearrange("p (c t) -> p c t", c=DC)[:, c, tc * 512:(tc + 1) * 512],
                            start=(c == 0), stop=(c == DC - 1)),
                            waits=wz0 if c == 0 else (), inc=s_pe if c == DC - 1 else None)
                    proj_tok[tc] = s_pe.n

                def rot(tc):
                    bkq = 2 + tc % 2
                    p_tok = proj_tok[tc]
                    w_ev = [(s_pe, p_tok)]
                    if tc == 0 and t2_guard is not None:
                        w_ev.append(t2_guard)
                    t_ev = on("scalar", lambda s, bkq=bkq, tc=tc, dstT=dstT, bias_ap=bias_ap: s.activation(
                        dstT[:, tc * 512:(tc + 1) * 512], banks[bkq][:],
                        AF.Identity, bias=bias_ap), waits=w_ev, inc=s_aa)
                    bank_free[bkq] = [(s_aa, t_ev)]
                    bkr = 4 + tc % 2
                    wz = [(s_aa, t_ev)] + bf_waits(bkr)
                    on("tensor", lambda t, bkr=bkr, tc=tc, dstT=dstT: t.matmul(
                        banks[bkr][0:64, :], permT[0:64, :],
                        dstT[0:64, tc * 512:(tc + 1) * 512],
                        start=True, stop=True, tile_position=(0, 0)),
                        waits=wz)
                    r_tok = on("tensor", lambda t, bkr=bkr, tc=tc, dstT=dstT: t.matmul(
                        banks[bkr][64:128, :], permT[64:128, :],
                        dstT[64:128, tc * 512:(tc + 1) * 512],
                        start=True, stop=True, tile_position=(64, 64)), inc=s_pe)
                    t_1 = on("vector", lambda v, bkr=bkr, tc=tc: v.tensor_mul(
                        T1[:, tc * 512:(tc + 1) * 512], banks[bkr][:],
                        sinT[:, tc * 512:(tc + 1) * 512]),
                        waits=[(s_pe, r_tok)], inc=s_vv)
                    t_2 = on("gpsimd", lambda g, tc=tc, dstT=dstT: g.tensor_mul(
                        T2[:, tc * 512:(tc + 1) * 512],
                        dstT[:, tc * 512:(tc + 1) * 512],
                        cosT[:, tc * 512:(tc + 1) * 512]),
                        waits=[(s_aa, t_ev), gn_free or (None, 0)] if tc == 0 else [(s_aa, t_ev)],
                        inc=s_gp)
                    t_c = on("vector", lambda v, tc=tc, dstT=dstT: v.tensor_add(
                        dstT[:, tc * 512:(tc + 1) * 512],
                        T1[:, tc * 512:(tc + 1) * 512],
                        T2[:, tc * 512:(tc + 1) * 512]),
                        waits=[(s_vv, t_1), (s_gp, t_2), (s_pe, r_tok)], inc=s_vv)
                    bank_free[bkr] = [(s_vv, t_1)]
                    toks[tc] = t_c

                proj(0); proj(1); rot(0); proj(2); rot(1); proj(3); rot(2); rot(3)
                return toks[3], s_pe.n

            qdone, _ = qk_rotary(0, tq, s_wq, QT, bias_q[:, pr:pr + 1],
                                 state.get("rot_t2_guard"))
            kdone, qk_mm_tok = qk_rotary(1, tk, s_wk, KT, bias_k[:, pr:pr + 1],
                                         (s_vv, qdone))
            state["rot_t2_guard"] = (s_vv, kdone)
            state["wqk_free"] = (s_pe, qk_mm_tok)

            for hl, h in ((0, h0), (1, h1)):
                wz_sg = [sg_free2] if sg_free2 is not None else []
                for tc in range(4):
                    bkg = 2 + tc % 2
                    for c in range(DC):
                        wz = []
                        if c == 0:
                            wz = [(s_wg2, tg)] + bf_waits(bkg)
                        on("tensor", lambda t, c=c, bkg=bkg, tc=tc, hl=hl: t.matmul(
                            banks[bkg][:],
                            WVG[:, 2048 + c * 256 + hl * 128: 2048 + c * 256 + hl * 128 + 128],
                            LNT[:].rearrange("p (c t) -> p c t", c=DC)[:, c, tc * 512:(tc + 1) * 512],
                            start=(c == 0), stop=(c == DC - 1)),
                            waits=wz, inc=s_pe if c == DC - 1 else None)
                    g_tok = s_pe.n
                    t_sg = on("scalar", lambda s, bkg=bkg, tc=tc, h=h: s.activation(
                        SG[:, tc * 512:(tc + 1) * 512], banks[bkg][:],
                        AF.Silu, bias=gb_f[:, h:h + 1]),
                        waits=[(s_pe, g_tok)] + wz_sg, inc=s_aa)
                    bank_free[bkg] = [(s_aa, t_sg)]
                sg_tok = s_aa.n
                last_g_mm = s_pe.n

                pp_free = [None, None]
                gnw = []
                for mb in range(4):
                    half = mb // 2
                    out_evacs = []
                    ntmax = 4 * mb + 3
                    mm1tok = {}

                    def emit_mm1(nt, mb=mb, hl=hl):
                        bks = S_BANKS[nt % 4]
                        wz = bf_waits(bks)
                        if nt == 0 and mb == 0:
                            wz = wz + [(s_vv, qdone), (s_vv, kdone),
                                       (s_vv, vev_toks[-1])]
                        mm1tok[nt] = on("tensor", lambda t, bks=bks, nt=nt, mb=mb, hl=hl: t.matmul(
                            banks[bks][:],
                            KT[hl * 64:(hl + 1) * 64, nt * 128:(nt + 1) * 128],
                            QT[hl * 64:(hl + 1) * 64, mb * 512:(mb + 1) * 512],
                            start=True, stop=True, tile_position=(hl * 64, 0)),
                            waits=wz, inc=s_pe)

                    for nt in range(min(MM1_LOOKAHEAD, ntmax + 1)):
                        emit_mm1(nt)
                    for nt in range(ntmax + 1):
                        bks = S_BANKS[nt % 4]
                        if MM1_LOOKAHEAD == 0 and nt == 0:
                            emit_mm1(0)
                        mm1 = mm1tok[nt]
                        slot = nt % 2
                        ms_d = nt - 4 * mb
                        wz_e = [(s_pe, mm1)]
                        if pp_free[slot] is not None:
                            wz_e.extend(pp_free[slot])
                        toks_e = []
                        if 0 <= ms_d <= 3:
                            toks_e.append(("vector", on("vector",
                                lambda v, bks=bks, slot=slot, ms_d=ms_d, h=h, nt=nt: v.scalar_tensor_tensor(
                                    PP[:, slot * 512 + ms_d * 128: slot * 512 + (ms_d + 1) * 128],
                                    banks[bks][:, ms_d * 128:(ms_d + 1) * 128],
                                    float(dsc[h][nt]),
                                    tril2[:, h * 128:(h + 1) * 128],
                                    ALU.mult, ALU.mult),
                                waits=wz_e, inc=s_vv)))
                        ms0 = max(0, ms_d + 1)
                        if ms0 <= 3:
                            toks_e.append(("scalar", on("scalar",
                                lambda s, bks=bks, slot=slot, ms0=ms0, h=h, nt=nt: s.activation(
                                    PP[:, slot * 512 + ms0 * 128: slot * 512 + 512],
                                    banks[bks][:, ms0 * 128:512],
                                    AF.Copy, scale=bg3[:, h * 16 + nt: h * 16 + nt + 1]),
                                waits=wz_e, inc=s_aa)))
                        bank_free[bks] = [((s_vv, t) if e == "vector" else (s_aa, t))
                                          for (e, t) in toks_e]
                        nxt = nt + (MM1_LOOKAHEAD if MM1_LOOKAHEAD else 1)
                        if nxt <= ntmax and nxt not in mm1tok:
                            emit_mm1(nxt)
                        wz_m = list(bank_free[bks])
                        first = True
                        msl = list(range(max(0, ms_d), 4))
                        for ms in msl:
                            msub = 4 * mb + ms
                            ob = OUT_BANKS[ms]
                            wzm = wz_m if first else []
                            if nt == 0 and first:
                                for b_ in OUT_BANKS:
                                    wzm = wzm + bf_waits(b_)
                            first = False
                            stop = (nt == 4 * mb + ms)
                            do_inc = stop or (ms == msl[-1])
                            last = on("tensor", lambda t, ob=ob, ms=ms, slot=slot, nt=nt, hl=hl, mb=mb, stop=stop: t.matmul(
                                banks[ob][:, 0:129],
                                PP[:, slot * 512 + ms * 128: slot * 512 + (ms + 1) * 128],
                                V3[:, hl, nt * 129:(nt + 1) * 129],
                                start=(nt == 0), stop=stop),
                                waits=wzm,
                                inc=s_pe if do_inc else None)
                            if stop:
                                out_evacs.append((ms, last))
                        pp_free[slot] = [(s_pe, s_pe.n)]
                    mtl0 = (mb % 2) * 4
                    done = {ms: t for (ms, t) in out_evacs}
                    oa_w = list(oa_free) if oa_free is not None else []
                    ev_toks = []
                    for ms in range(4):
                        ob = OUT_BANKS[ms]
                        acol = alpha[:, h * 16 + 4 * mb + ms: h * 16 + 4 * mb + ms + 1]
                        wz = [(s_pe, done[ms])] + (oa_w if ms == 0 else [])
                        if ms % 2 == 0:
                            t_o = on("vector", lambda v, ob=ob, mtl0=mtl0, ms=ms, acol=acol: v.tensor_scalar(
                                OA[:, (mtl0 + ms) * 129:(mtl0 + ms + 1) * 129],
                                banks[ob][:, 0:129], acol, None, ALU.mult),
                                waits=wz, inc=s_vv)
                            bank_free[ob] = [(s_vv, t_o)]
                            ev_toks.append(("vector", t_o))
                        else:
                            t_o = on("scalar", lambda s, ob=ob, mtl0=mtl0, ms=ms, acol=acol: s.activation(
                                OA[:, (mtl0 + ms) * 129:(mtl0 + ms + 1) * 129],
                                banks[ob][:, 0:129], AF.Copy, scale=acol),
                                waits=wz, inc=s_aa)
                            bank_free[ob] = [(s_aa, t_o)]
                            ev_toks.append(("scalar", t_o))
                    t_o1 = [t for (e, t) in ev_toks if e == "vector"][-1]
                    t_o2 = [t for (e, t) in ev_toks if e == "scalar"][-1]
                    if mb % 2 == 0:
                        continue
                    evw = [(s_vv, t_o1), (s_aa, t_o2)]
                    e1 = on("vector", lambda v: v.tensor_copy(
                        ep_rs[:, 0:8], OA3[:, :, 128:129]), waits=evw, inc=s_vv)
                    e7 = on("vector", lambda v: v.reduce_sum(
                        ep_m1[:, 0:8], OA3[:, :, 0:128], axis=AX.X), inc=s_vv)
                    e2 = on("scalar", lambda s: s.activation(
                        ep_d[:, 0:8], ep_rs[:, 0:8], AF.Abs),
                        waits=[(s_vv, e1)], inc=s_aa)
                    e4 = on("vector", lambda v: v.tensor_scalar(
                        ep_d[:, 0:8], ep_d[:, 0:8], 1.0, None, ALU.max),
                        waits=[(s_aa, e2)], inc=s_vv)
                    e5 = on("vector", lambda v: v.scalar_tensor_tensor(
                        ep_beta[:, 0:8], ep_d[:, 0:8], EPS, ep_d[:, 0:8],
                        ALU.mult, ALU.mult), waits=[(s_vv, e4)], inc=s_vv)
                    w8 = list(evw)
                    if state["junk2_guard"] is not None:
                        w8.extend(state["junk2_guard"])
                        state["junk2_guard"] = None
                    e8 = on("scalar", lambda s: s.activation(
                        junk3, OA3[:, :, 0:128], AF.Square), waits=w8, inc=s_aa)
                    state["junk2_free"] = (s_aa, e8)
                    e9 = on("vector", lambda v: v.reduce_sum(
                        ep_m2[:, 0:8], junk3, axis=AX.X), waits=[(s_aa, e8)], inc=s_vv)
                    e10 = on("vector", lambda v: v.scalar_tensor_tensor(
                        ep_rs[:, 0:8], ep_m1[:, 0:8], 1.0 / 128, ep_m1[:, 0:8],
                        ALU.mult, ALU.mult), waits=[(s_vv, e7)], inc=s_vv)
                    e11 = on("vector", lambda v: v.tensor_sub(
                        ep_sc[:, 0:8], ep_m2[:, 0:8], ep_rs[:, 0:8]),
                        waits=[(s_vv, e10), (s_vv, e9)], inc=s_vv)
                    e11b = on("vector", lambda v: v.scalar_tensor_tensor(
                        ep_bi[:, 0:8], ep_sc[:, 0:8], 1.0 / 128, ep_beta[:, 0:8],
                        ALU.mult, ALU.add), waits=[(s_vv, e11), (s_vv, e5)], inc=s_vv)
                    e12 = on("scalar", lambda s: s.activation(
                        ep_m2[:, 0:8], ep_bi[:, 0:8], AF.Ln),
                        waits=[(s_vv, e11b)], inc=s_aa)
                    e13 = on("scalar", lambda s: s.activation(
                        ep_m2[:, 0:8], ep_m2[:, 0:8], AF.Exp, scale=-0.5),
                        waits=[(s_aa, e12)], inc=s_aa)
                    e15 = on("vector", lambda v: v.scalar_tensor_tensor(
                        ep_bi[:, 0:8], ep_m1[:, 0:8], -1.0 / 128, ep_m2[:, 0:8],
                        ALU.mult, ALU.mult), waits=[(s_aa, e13)], inc=s_vv)
                    lastf = None
                    for mtl in range(8):
                        wz = [(s_vv, e15)] if mtl == 0 else []
                        if mtl == 0 and gn_free is not None:
                            wz.append(gn_free)
                        lastf = on("vector", lambda v, mtl=mtl, half=half: v.tensor_scalar(
                            T2[:, (half * 8 + mtl) * 128:(half * 8 + mtl + 1) * 128],
                            OA3[:, mtl, 0:128],
                            ep_m2[:, mtl:mtl + 1], ep_bi[:, mtl:mtl + 1],
                            ALU.mult, ALU.add), waits=wz, inc=s_vv)
                    oa_free = [(s_vv, lastf), (s_aa, e8)]
                    gnw.append(lastf)
                for grp in range(2):
                    bk = grp
                    tokp = None
                    for j in range(8):
                        mt = grp * 8 + j
                        wz = []
                        if j == 0:
                            wz = bf_waits(bk) + [(s_vv, gnw[grp])]
                            if grp == 0:
                                wz.append((s_aa, sg_tok))
                        tokp = on("tensor", lambda t, bk=bk, mt=mt: t.transpose(
                            bank_bf16(bk)[:, (mt % 8) * 128:((mt % 8) + 1) * 128],
                            T2[:, mt * 128:(mt + 1) * 128], ident[:]),
                            waits=wz, inc=s_pe if j == 7 else None)
                    wy = [(s_pe, tokp)]
                    if last_ymul is None and pr == 0:
                        wy.extend(cast_done)
                    t_y = on("vector", lambda v, bk=bk, grp=grp, h=h: v.tensor_mul(
                        yT[:, h * 2048 + grp * 1024: h * 2048 + (grp + 1) * 1024],
                        bank_bf16(bk)[:],
                        SG[:, grp * 1024:(grp + 1) * 1024]),
                        waits=wy, inc=s_vv)
                    bank_free[bk] = [(s_vv, t_y)]
                    last_ymul = t_y
                gn_free = (s_pe, s_pe.n)
                sg_free2 = (s_vv, last_ymul)
            state["wvg_free"] = (s_pe, last_g_mm)
        proj_done = (s_pe, s_pe.n)
        for oc4 in range(4):
            w = [proj_done, cast_tok[("ow", oc4 * 4)], cast_tok[("ow", oc4 * 4 + 1)],
                 cast_tok[("ow", oc4 * 4 + 2)], cast_tok[("ow", oc4 * 4 + 3)]]
            dma("sync",
                LNT[:, oc4 * 4096:(oc4 + 1) * 4096].rearrange("p (a b) -> p a b", a=4),
                s_ow_d[oc4 * 4:oc4 * 4 + 4].transpose([1, 0, 2]), s_ow, waits=w)
        ow_tok = s_ow.n
        for tt in range(NTT):
            sl = tt % 2
            w = [wait_src_store[tt]] if wait_src_store is not None else []
            if state["resid_free"][sl] is not None:
                w.append(state["resid_free"][sl])
            if tt == 0 and state["junk2_free"] is not None:
                w.append(state["junk2_free"])
                state["junk2_free"] = None
            r_tok = dma("sync", xrs_sb[:, sl * D:(sl + 1) * D],
                        src_x[tt * 128:(tt + 1) * 128, :], s_dr[sl], waits=w)
            e_tok = None
            for dc in range(2):
                bk = (2 if tt % 2 == 0 else 4) + dc
                for fc in range(NH):
                    wz = []
                    if fc == 0:
                        wz = bf_waits(bk) + [(s_ow, ow_tok)]
                        if tt == 0 and dc == 0:
                            wz.append((s_vv, last_ymul))
                    on("tensor", lambda t, fc=fc, bk=bk, tt=tt, dc=dc: t.matmul(
                        banks[bk][:],
                        yT[:, fc * 2048 + tt * 128: fc * 2048 + (tt + 1) * 128],
                        LNT[:, fc * 1024 + dc * 512: fc * 1024 + (dc + 1) * 512],
                        start=(fc == 0), stop=(fc == NH - 1)),
                        waits=wz, inc=s_pe if fc == NH - 1 else None)
                o_tok = s_pe.n
                wz = [(s_pe, o_tok)]
                if xpo_free[sl] is not None:
                    wz.append(xpo_free[sl])
                if lnv_free[sl] is not None:
                    wz.append(lnv_free[sl])
                    if dc == 1:
                        lnv_free[sl] = None
                if tt == 0 and dc == 0:
                    wz.append((s_dc2, consts2_tok))
                t_b1 = on("vector", lambda v, bk=bk, sl=sl, dc=dc: v.tensor_tensor(
                    xpo_sb[:, sl * D + dc * 512: sl * D + (dc + 1) * 512],
                    banks[bk][:], ob_rep[:, dc * 512:(dc + 1) * 512], ALU.add),
                    waits=wz, inc=s_vv)
                e_tok = on("vector", lambda v, sl=sl, dc=dc: v.tensor_add(
                    xpo_sb[:, sl * D + dc * 512: sl * D + (dc + 1) * 512],
                    xpo_sb[:, sl * D + dc * 512: sl * D + (dc + 1) * 512],
                    xrs_sb[:, sl * D + dc * 512: sl * D + (dc + 1) * 512]),
                    waits=[(s_dr[sl], r_tok), (s_vv, t_b1)], inc=s_vv)
                bank_free[bk] = [(s_vv, e_tok)]
            state["resid_free"][sl] = (s_vv, e_tok)
            st_tok = dma("sync", dst_x[tt * 128:(tt + 1) * 128, :],
                         xpo_sb[:, sl * D:(sl + 1) * D], s_st[sl],
                         waits=[(s_vv, e_tok)])
            xpo_free[sl] = (s_st[sl], st_tok)
            store_tile_tok[tt] = (s_st[sl], st_tok)
        state["yT_read_done"] = (s_pe, s_pe.n)
        state["arenaB_free"] = [(s_pe, s_pe.n), (s_vv, s_vv.n),
                                (s_aa, s_aa.n), (s_gp, s_gp.n)]
        return list(store_tile_tok)

    for b in range(B):
        A, S, O = x_in[b], x_scr[b], x_out[b]
        if N_SUBLAYERS == 1:
            emit_ffn(A, O, wait_src_store=None)
        elif N_SUBLAYERS == 2:
            tok1 = emit_ffn(A, S, wait_src_store=None)
            emit_attn(S, O, wait_src_store=tok1)
        else:
            tok1 = emit_ffn(A, S, wait_src_store=None)
            tok2 = emit_attn(S, O, wait_src_store=tok1)
            emit_ffn(O, O, wait_src_store=tok2,
                     tail_guard=state["yT_read_done"])

    P.emit("sync", lambda e, n0=s_st[0].n, n1=s_st[1].n: (
        e.wait_ge(s_st[0].h, n0), e.wait_ge(s_st[1].h, n1)))

    with nc.Block() as block:
        P.realize(block)
    stack.close()
    return nc



def kernel(**inputs):
    tabs = host_tables()
    wts = retile_weights(inputs)
    x = np.asarray(inputs["L0_attn_x"])

    nc = build_program(tabs)

    in_maps = []
    for c in range(NCORES):
        in_maps.append({
            "x_in": np.ascontiguousarray(x[c * B:(c + 1) * B]),
            "wgu_t": wts["wgu_t"], "wd_t": wts["wd_t"],
            "qw_t": wts["qw_t"], "kw_t": wts["kw_t"],
            "vw_t": wts["vw_t"], "gw_t": wts["gw_t"], "ow_t": wts["ow_t"],
            "q_b": wts["q_b"], "k_b": wts["k_b"], "v_b": wts["v_b"],
            "g_b": wts["g_b"], "o_b": wts["o_b"],
            "c_sinT": tabs["sinT"], "c_cosT": tabs["cosT"],
            "c_permT": tabs["permT"], "c_ident": tabs["ident"],
            "c_identf": tabs["identf"],
            "c_bg3": tabs["bg3"], "c_tril2": tabs["tril2"],
            "c_alpha": tabs["alpha"],
        })

    res = run_bass_kernel_spmd(nc, in_maps, core_ids=list(range(NCORES)))
    return np.concatenate([r["x_out"] for r in res.results], axis=0)


# version 08
# speedup vs baseline: 1.0337x; 1.0013x over previous
import contextlib
import numpy as np
import ml_dtypes
import concourse.bass as bass
import concourse.mybir as mybir
from concourse.bass_utils import run_bass_kernel_spmd

F32 = mybir.dt.float32
BF16 = mybir.dt.bfloat16
I32 = mybir.dt.int32
AF = mybir.ActivationFunctionType
ALU = mybir.AluOpType
AX = mybir.AxisListType
BF = ml_dtypes.bfloat16

NCORES = 8
B = 4
T = 2048
D = 1024
HID = 3072
NHC = HID // 128
DC = D // 128
NTT = T // 128
NH = 16
EPS = 1e-5
QDEQ = 1.0 / 127.0
DOWN_SCALE = 0.5 * QDEQ * QDEQ

N_SUBLAYERS = 3
MM1_LOOKAHEAD = 3



def host_tables():
    t = {}
    half = 1.0 / np.power(10000.0, np.linspace(0.0, 1.0, 32))
    angle = np.repeat(half, 2)
    idx = np.arange(T, dtype=np.float64)
    sinT = np.sin(idx[:, None] * angle[None, :]).T.astype(np.float32)
    cosT = np.cos(idx[:, None] * angle[None, :]).T.astype(np.float32)
    t["sinT"] = np.ascontiguousarray(np.concatenate([sinT, sinT], 0)).astype(BF)
    t["cosT"] = np.ascontiguousarray(np.concatenate([cosT, cosT], 0)).astype(BF)
    s64 = np.zeros((64, 64), dtype=np.float32)
    for i in range(32):
        s64[2 * i + 1, 2 * i] = -1.0
        s64[2 * i, 2 * i + 1] = 1.0
    t["permT"] = np.ascontiguousarray(np.concatenate([s64, s64], 0)).astype(BF)
    t["ident"] = np.eye(128, dtype=np.float32).astype(BF)
    t["identf"] = np.eye(128, dtype=np.float32)
    dec = np.log(1.0 - 2.0 ** (-5.0 - np.arange(NH, dtype=np.float64)))
    n_ = np.arange(128, dtype=np.float64)
    bg3 = 0.125 * np.exp(-dec[None, :, None]
                         * (128.0 * np.arange(16)[None, None, :] + n_[:, None, None]))
    t["bg3"] = np.ascontiguousarray(bg3.reshape(128, NH * 16)).astype(np.float32)
    m_ = np.arange(128)
    trilmask = (m_[None, :] >= n_[:, None]).astype(np.float64)
    tril2 = np.stack([trilmask * 0.125 * np.exp(-dec[h] * n_)[:, None]
                      for h in range(NH)])
    t["tril2"] = tril2.astype(np.float32)
    mg = np.arange(T, dtype=np.float64)
    alpha = np.zeros((128, NH * 16), dtype=np.float64)
    for h in range(NH):
        csum = np.cumsum(np.exp(dec[h] * mg))
        am = np.exp(dec[h] * mg) / np.sqrt(csum)
        alpha[:, h * 16:(h + 1) * 16] = am.reshape(16, 128).T
    t["alpha"] = alpha.astype(np.float32)
    t["diag_scale"] = np.exp(-dec[:, None] * 128.0 * np.arange(16)[None, :])
    return t


def retile_weights(ins):
    out = {}
    wg = np.asarray(ins["L0_ffn_w_gate"])
    wu = np.asarray(ins["L0_ffn_w_up"])
    wd = np.asarray(ins["L0_ffn_w_down"])

    def tile_gate(w):
        return np.ascontiguousarray(w.reshape(NHC, 128, DC, 128).transpose(0, 3, 2, 1))

    out["wgu_t"] = np.ascontiguousarray(
        np.stack([tile_gate(wg), tile_gate(wu)], axis=1))
    out["wd_t"] = np.ascontiguousarray(
        wd.reshape(D, NHC, 128).transpose(1, 2, 0))

    def tile_qk(w):
        return np.ascontiguousarray(
            w.reshape(DC, 128, NH // 2, 128).transpose(2, 1, 0, 3))

    out["qw_t"] = tile_qk(np.asarray(ins["L0_attn_q_w"]))
    out["kw_t"] = tile_qk(np.asarray(ins["L0_attn_k_w"]))

    def tile_vg(w):
        return np.ascontiguousarray(
            w.reshape(DC, 128, NH // 2, 256).transpose(2, 1, 0, 3))

    out["vw_t"] = tile_vg(np.asarray(ins["L0_attn_v_w"]))
    out["gw_t"] = tile_vg(np.asarray(ins["L0_attn_g_w"]))
    out["ow_t"] = np.ascontiguousarray(np.asarray(ins["L0_attn_o_w"])).reshape(NH, 128, D)
    for k in ("q_b", "k_b", "v_b", "g_b", "o_b"):
        out[k] = np.ascontiguousarray(np.asarray(ins["L0_attn_" + k]))
    return out



class Sem:
    def __init__(self, stack, nc, name):
        self.h = stack.enter_context(nc.semaphore(name))
        self.n = 0

    def bump(self, k=1):
        self.n += k
        return self.n


class Prog:
    ENGINES = ("sync", "tensor", "vector", "scalar", "gpsimd")

    def __init__(self):
        self.streams = {e: [] for e in self.ENGINES}

    def emit(self, eng, fn):
        self.streams[eng].append(fn)

    def realize(self, block):
        for name in self.ENGINES:
            fns = self.streams[name]
            getattr(block, name)(lambda eng, fns=fns: [f(eng) for f in fns])


def build_program(tabs):
    nc = bass.Bass("TRN2", target_bir_lowering=False)
    stack = contextlib.ExitStack()

    x_in = nc.dram_tensor("x_in", [B, T, D], F32, kind="ExternalInput").ap()
    x_out = nc.dram_tensor("x_out", [B, T, D], F32, kind="ExternalOutput").ap()
    x_scr = nc.dram_tensor("x_scr", [B, T, D], F32, kind="Internal").ap()

    wgu_t = nc.dram_tensor("wgu_t", [NHC, 2, 128, DC, 128], I32, kind="ExternalInput").ap()
    wd_t = nc.dram_tensor("wd_t", [NHC, 128, D], I32, kind="ExternalInput").ap()
    qw_t = nc.dram_tensor("qw_t", [NH // 2, 128, DC, 128], F32, kind="ExternalInput").ap()
    kw_t = nc.dram_tensor("kw_t", [NH // 2, 128, DC, 128], F32, kind="ExternalInput").ap()
    vw_t = nc.dram_tensor("vw_t", [NH // 2, 128, DC, 256], F32, kind="ExternalInput").ap()
    gw_t = nc.dram_tensor("gw_t", [NH // 2, 128, DC, 256], F32, kind="ExternalInput").ap()
    ow_t = nc.dram_tensor("ow_t", [NH, 128, D], F32, kind="ExternalInput").ap()
    q_b = nc.dram_tensor("q_b", [D], F32, kind="ExternalInput").ap()
    k_b = nc.dram_tensor("k_b", [D], F32, kind="ExternalInput").ap()
    v_b = nc.dram_tensor("v_b", [2 * D], F32, kind="ExternalInput").ap()
    g_b = nc.dram_tensor("g_b", [2 * D], F32, kind="ExternalInput").ap()
    o_b = nc.dram_tensor("o_b", [D], F32, kind="ExternalInput").ap()

    c_sinT = nc.dram_tensor("c_sinT", [128, T], BF16, kind="ExternalInput").ap()
    c_cosT = nc.dram_tensor("c_cosT", [128, T], BF16, kind="ExternalInput").ap()
    c_permT = nc.dram_tensor("c_permT", [128, 64], BF16, kind="ExternalInput").ap()
    c_ident = nc.dram_tensor("c_ident", [128, 128], BF16, kind="ExternalInput").ap()
    c_identf = nc.dram_tensor("c_identf", [128, 128], F32, kind="ExternalInput").ap()
    c_bg3 = nc.dram_tensor("c_bg3", [128, NH * 16], F32, kind="ExternalInput").ap()
    c_tril2 = nc.dram_tensor("c_tril2", [NH, 128, 128], F32, kind="ExternalInput").ap()
    c_alpha = nc.dram_tensor("c_alpha", [128, NH * 16], F32, kind="ExternalInput").ap()

    s_wgu = nc.dram_tensor("s_wgu", [NHC, 2, 128, DC * 128], BF16, kind="Internal").ap()
    s_wd = nc.dram_tensor("s_wd", [NHC, 128, D], BF16, kind="Internal").ap()
    s_qw = nc.dram_tensor("s_qw", [NH // 2, 128, DC * 128], BF16, kind="Internal").ap()
    s_kw = nc.dram_tensor("s_kw", [NH // 2, 128, DC * 128], BF16, kind="Internal").ap()
    s_vw = nc.dram_tensor("s_vw", [NH // 2, 128, DC * 256], BF16, kind="Internal").ap()
    s_gw = nc.dram_tensor("s_gw", [NH // 2, 128, DC * 256], BF16, kind="Internal").ap()
    s_ow_d = nc.dram_tensor("s_ow", [NH, 128, D], BF16, kind="Internal").ap()

    def sb(name, shape, dt):
        return stack.enter_context(nc.sbuf_tensor(name, shape, dt))

    LNT = sb("LNT", [128, DC * T], BF16)
    arenaA = sb("arenaA", [128, 16 * T], BF16)
    arenaB = sb("arenaB", [128, NHC * D], BF16)
    wgu_sb = sb("wgu_sb", [128, 2 * 2048], BF16)
    sg_sb = sb("sg_sb", [128, 1024], BF16)
    xin_sb = sb("xin_sb", [128, 2 * D], F32)
    xrs_sb = sb("xrs_sb", [128, 2 * D], F32)
    xpo_sb = sb("xpo_sb", [128, 2 * D], F32)
    st_s1 = sb("st_s1", [128, 16], F32)
    st_s2 = sb("st_s2", [128, 16], F32)
    st_u = sb("st_u", [128, 16], F32)
    st_v = sb("st_v", [128, 16], F32)
    st_r = sb("st_r", [128, 16], F32)
    st_nb = sb("st_nb", [128, 16], F32)
    sinT = sb("sinT", [128, T], BF16)
    cosT = sb("cosT", [128, T], BF16)
    permT = sb("permT", [128, 64], BF16)
    ident = sb("ident", [128, 128], BF16)

    bg3 = sb("bg3", [128, NH * 16], F32)
    tril2 = sb("tril2", [128, NH * 128], F32)
    alpha = sb("alpha", [128, NH * 16], F32)
    bias_q = sb("bias_q", [128, DC], F32)
    bias_k = sb("bias_k", [128, DC], F32)
    gb_f = sb("gb_f", [128, NH], F32)
    vb_rep = sb("vb_rep", [128, 2 * D], BF16)
    ob_rep = sb("ob_rep", [128, D], BF16)
    bstage = sb("bstage", [128, 128], F32)
    ep_rs = sb("ep_rs", [128, 16], F32)
    ep_d = sb("ep_d", [128, 16], F32)
    ep_beta = sb("ep_beta", [128, 16], F32)
    ep_m1 = sb("ep_m1", [128, 16], F32)
    ep_m2 = sb("ep_m2", [128, 16], F32)
    ep_sc = sb("ep_sc", [128, 16], F32)
    ep_bi = sb("ep_bi", [128, 16], F32)
    c_eps = sb("c_eps", [128, 1], F32)

    junk_sb2 = [arenaA[:, 28672:30720].bitcast(F32)[:, 0:D],
                arenaA[:, 30720:32768].bitcast(F32)[:, 0:D]]

    identf = xpo_sb[:, 0:128]

    banks = [stack.enter_context(nc.psum_tensor(f"pb{i}", [128, 512], F32))
             for i in range(8)]

    def bank_bf16(i):
        return banks[i][:].bitcast(BF16)

    s_dc = Sem(stack, nc, "s_dc")
    s_dc2 = Sem(stack, nc, "s_dc2")
    s_bs = Sem(stack, nc, "s_bs")
    s_cl = [Sem(stack, nc, f"s_cl{i}") for i in range(2)]
    s_cs = [Sem(stack, nc, f"s_cs{i}") for i in range(2)]
    s_wg = [Sem(stack, nc, f"s_wg{i}") for i in range(2)]
    s_wdT = Sem(stack, nc, "s_wdT")
    s_dx = [Sem(stack, nc, f"s_dx{i}") for i in range(2)]
    s_dr = [Sem(stack, nc, f"s_dr{i}") for i in range(2)]
    s_st = [Sem(stack, nc, f"s_st{i}") for i in range(2)]
    s_wq = Sem(stack, nc, "s_wq")
    s_wk = Sem(stack, nc, "s_wk")
    s_wv = Sem(stack, nc, "s_wv")
    s_wg2 = Sem(stack, nc, "s_wg2")
    s_ow = Sem(stack, nc, "s_ow")
    s_pe = Sem(stack, nc, "s_pe")
    s_vv = Sem(stack, nc, "s_vv")
    s_aa = Sem(stack, nc, "s_aa")
    s_gp = Sem(stack, nc, "s_gp")

    P = Prog()

    def on(eng, fn, waits=(), inc=None, inc_n=1):
        waits = tuple((s, v) for (s, v) in waits if s is not None)
        tok = inc.bump(inc_n) if inc is not None else None

        def f(e, fn=fn, waits=waits, inc=inc, inc_n=inc_n):
            for (s, v) in waits:
                e.wait_ge(s.h, v)
            r = fn(e)
            if inc is not None:
                r.then_inc(inc.h, inc_n)
        P.emit(eng, f)
        return tok

    def dma(queue, dst, src, sem, waits=()):
        waits = tuple((s, v) for (s, v) in waits if s is not None)
        tok = sem.bump(16)

        def f(e, dst=dst, src=src, sem=sem, waits=waits):
            for (s, v) in waits:
                e.wait_ge(s.h, v)
            e.dma_start(out=dst, in_=src).then_inc(sem.h, 16)
        P.emit(queue, f)
        return tok

    bank_free = [None] * 8

    def bf_waits(i):
        return list(bank_free[i]) if bank_free[i] is not None else []

    dma("sync", sinT[:], c_sinT, s_dc)
    dma("sync", cosT[:], c_cosT, s_dc)
    dma("sync", permT[:], c_permT, s_dc)
    dma("sync", ident[:], c_ident, s_dc)
    dma("sync", identf, c_identf, s_dc)
    dma("sync", bg3[:], c_bg3, s_dc)
    dma("sync", alpha[:], c_alpha, s_dc)
    for h0 in range(0, NH, 4):
        dma("sync", tril2[:].rearrange("p (h m) -> p h m", h=NH)[:, h0:h0 + 4, :],
            c_tril2[h0:h0 + 4].transpose([1, 0, 2]), s_dc)
    consts_tok = s_dc.n
    dma("gpsimd", vb_rep[:], v_b[None, :].broadcast_to([128, 2 * D]), s_dc2)
    dma("gpsimd", ob_rep[:], o_b[None, :].broadcast_to([128, D]), s_dc2)
    consts2_tok = s_dc2.n

    t_eps = on("vector", lambda v: v.memset(c_eps[:], EPS), inc=s_vv)
    on("scalar", lambda s: s.wait_ge(s_vv.h, t_eps))
    t_z = on("vector", lambda v: v.memset(bstage[:], 0.0), inc=s_vv)
    small_b = [(q_b, DC, bias_q), (k_b, DC, bias_k), (g_b, NH, gb_f)]
    for i, (src, rows, dst) in enumerate(small_b):
        t_b = dma("sync", bstage[0:rows, :], src.rearrange("(c p) -> c p", p=128),
                  s_bs, waits=[(s_vv, s_vv.n)])
        t_t = on("tensor", lambda t: t.transpose(banks[0][:, 0:128], bstage[:], identf),
                 waits=[(s_bs, t_b), (s_dc, consts_tok)] + bf_waits(0), inc=s_pe)
        t_c = on("vector", lambda v, dst=dst, rows=rows: v.tensor_copy(
            dst[:], banks[0][:, 0:rows]), waits=[(s_pe, t_t)], inc=s_vv)
        bank_free[0] = [(s_vv, t_c)]
        if i < len(small_b) - 1:
            on("vector", lambda v, rows=rows: v.memset(bstage[0:rows, :], 0.0), inc=s_vv)
    bias_done = s_vv.n

    cast_jobs = []
    for hc in range(NHC):
        cast_jobs.append(("wgu", hc, wgu_t[hc].rearrange("g p c j -> p g (c j)"),
                          s_wgu[hc].rearrange("g p f -> p g f"), 2048))
    for hc in range(NHC):
        cast_jobs.append(("wd", hc, wd_t[hc], s_wd[hc], 1024))
    for pr in range(NH // 2):
        cast_jobs.append(("qw", pr, qw_t[pr].rearrange("p c j -> p (c j)"), s_qw[pr], 1024))
        cast_jobs.append(("kw", pr, kw_t[pr].rearrange("p c j -> p (c j)"), s_kw[pr], 1024))
    for pr in range(NH // 2):
        cast_jobs.append(("vw", pr, vw_t[pr].rearrange("p c j -> p (c j)"), s_vw[pr], 2048))
        cast_jobs.append(("gw", pr, gw_t[pr].rearrange("p c j -> p (c j)"), s_gw[pr], 2048))
    for h in range(NH):
        cast_jobs.append(("ow", h, ow_t[h], s_ow_d[h], 1024))

    cast_tok = {}
    store_toks = []
    for i, (kind, idx, src, dst, width) in enumerate(cast_jobs):
        slot = i % 2
        sl = arenaA[:, 24576 + slot * 2048: 24576 + slot * 2048 + width]
        if kind == "wgu":
            sl = sl.rearrange("p (g f) -> p g f", g=2)
        w = []
        if i >= 2:
            w.append((s_cs[slot], store_toks[i - 2]))
        g_tok = dma("gpsimd", sl, src, s_cl[slot], waits=w)
        st = dma("sync", dst, sl, s_cs[slot], waits=[(s_cl[slot], g_tok)])
        store_toks.append(st)
        cast_tok[(kind, idx)] = (s_cs[slot], st)
    cast_done = [(s_cs[0], s_cs[0].n), (s_cs[1], s_cs[1].n)]

    apply_toks = [None, None]
    xpo_free = [None, None]
    ln_tile_tok = [0] * NTT
    store_tile_tok = [None] * NTT
    lnv_free = [None, None]
    sg_free = [None, None]

    def emit_ln(src_x, wait_store_tok=None, tail_guard=None):
        GS = 4
        rd_toks = {}
        grp_math = {}
        slot_prev = state["xin_slot_prev"]

        def pass_a(g):
            for tt in range(g * GS, (g + 1) * GS):
                sl = tt % 2
                w = [wait_store_tok[tt]] if wait_store_tok is not None else []
                w.extend(slot_prev[sl])
                xt_tok = dma("sync", xin_sb[:, sl * D:(sl + 1) * D],
                             src_x[tt * 128:(tt + 1) * 128, :], s_dx[sl], waits=w)
                xin = xin_sb[:, sl * D:(sl + 1) * D]
                t_r = on("vector", lambda v, xin=xin, tt=tt: v.reduce_sum(
                    st_s1[:, tt:tt + 1], xin, axis=AX.X),
                    waits=[(s_dx[sl], xt_tok)], inc=s_vv)
                w_sq = [(s_dx[sl], xt_tok)]
                if tt == 0 and tail_guard is not None:
                    w_sq.append(tail_guard)
                t_q = on("scalar", lambda s, xin=xin, tt=tt: s.activation(
                    junk_sb2[tt % 2], xin, AF.Square, accum_out=st_s2[:, tt:tt + 1]),
                    waits=w_sq, inc=s_aa)
                rd_toks[tt] = [(s_vv, t_r), (s_aa, t_q)]
                slot_prev[sl] = [(s_vv, t_r), (s_aa, t_q)]

        def pass_b(g):
            sl4 = slice(g * GS, (g + 1) * GS)
            last_red = rd_toks[(g + 1) * GS - 1][0][1]
            last_sq = rd_toks[(g + 1) * GS - 1][1][1]
            t_u = on("vector", lambda v: v.scalar_tensor_tensor(
                st_u[:, sl4], st_s1[:, sl4], 1.0 / D, st_s1[:, sl4],
                ALU.mult, ALU.mult), waits=[(s_vv, last_red)], inc=s_vv)
            t_vd = on("vector", lambda v: v.tensor_sub(
                st_v[:, sl4], st_s2[:, sl4], st_u[:, sl4]),
                waits=[(s_vv, t_u), (s_aa, last_sq)], inc=s_vv)
            t_l = on("scalar", lambda s: s.activation(
                st_r[:, sl4], st_v[:, sl4], AF.Ln, scale=1.0 / D, bias=c_eps[:, 0:1]),
                waits=[(s_vv, t_vd)], inc=s_aa)
            t_e = on("scalar", lambda s: s.activation(
                st_r[:, sl4], st_r[:, sl4], AF.Exp, scale=-0.5),
                waits=[(s_aa, t_l)], inc=s_aa)
            t_nb = on("vector", lambda v: v.scalar_tensor_tensor(
                st_nb[:, sl4], st_s1[:, sl4], -1.0 / D, st_r[:, sl4],
                ALU.mult, ALU.mult), waits=[(s_aa, t_e)], inc=s_vv)
            grp_math[g] = t_nb

        def pass_c(g):
            t_nb = grp_math[g]
            for tt in range(g * GS, (g + 1) * GS):
                sl = tt % 2
                w = list(slot_prev[sl])
                if wait_store_tok is not None:
                    w.append(wait_store_tok[tt])
                xt_tok = dma("sync", xin_sb[:, sl * D:(sl + 1) * D],
                             src_x[tt * 128:(tt + 1) * 128, :], s_dx[sl], waits=w)
                xin = xin_sb[:, sl * D:(sl + 1) * D]
                lnv = xpo_sb[:, sl * D:(sl + 1) * D].bitcast(BF16)[:, 0:D]
                w_ap = [(s_vv, t_nb), (s_dx[sl], xt_tok)]
                if xpo_free[sl] is not None:
                    w_ap.append(xpo_free[sl])
                if lnv_free[sl] is not None:
                    w_ap.append(lnv_free[sl])
                t_ap = on("vector", lambda v, xin=xin, lnv=lnv, tt=tt: v.tensor_scalar(
                    lnv, xin, st_r[:, tt:tt + 1], st_nb[:, tt:tt + 1],
                    ALU.mult, ALU.add), waits=w_ap, inc=s_vv)
                apply_toks[sl] = (s_vv, t_ap)
                slot_prev[sl] = [(s_vv, t_ap)]
                bk = 6 + (tt % 2)
                tokp = None
                for c in range(DC):
                    w2 = []
                    if c == 0:
                        w2 = [(s_vv, t_ap)] + bf_waits(bk)
                    tokp = on("tensor", lambda t, bk=bk, c=c, lnv=lnv: t.transpose(
                        bank_bf16(bk)[:, c * 128:(c + 1) * 128],
                        lnv[:, c * 128:(c + 1) * 128], ident[:]),
                        waits=w2, inc=s_pe if c == DC - 1 else None)
                lnv_free[sl] = (s_pe, tokp)
                dst = LNT[:].rearrange("p (c t) -> p c t", c=DC)[:, :, tt * 128:(tt + 1) * 128]
                t_ev = on("vector", lambda v, bk=bk, dst=dst: v.tensor_copy(
                    dst, bank_bf16(bk)[:].rearrange("p (c t) -> p c t", c=DC)),
                    waits=[(s_pe, tokp)], inc=s_vv)
                bank_free[bk] = [(s_vv, t_ev)]
                ln_tile_tok[tt] = t_ev
        pass_a(0)
        pass_a(1)
        pass_b(0)
        pass_c(0)
        pass_a(2)
        pass_b(1)
        pass_c(1)
        pass_a(3)
        pass_b(2)
        pass_c(2)
        pass_b(3)
        pass_c(3)
        return ln_tile_tok[NTT - 1]

    H_view = arenaA
    wdT_sb = arenaB
    state = {"wgu_slot_free": [None, None], "arenaB_free": None,
             "resid_free": [None, None], "yT_read_done": None,
             "wqk_free": None, "wvg_free": None,
             "junk2_free": None, "junk2_guard": None,
             "xin_slot_prev": [[], []]}

    def emit_ffn(src_x, dst_x, wait_src_store=None, tail_guard=None):
        def load_wgu(hc):
            slot = hc % 2
            w = [cast_tok[("wgu", hc)]]
            if state["wgu_slot_free"][slot] is not None:
                w.append((s_pe, state["wgu_slot_free"][slot]))
            return dma("sync",
                       wgu_sb[:, slot * 2048:(slot + 1) * 2048].rearrange(
                           "p (g f) -> p g f", g=2),
                       s_wgu[hc].rearrange("g p f -> p g f"), s_wg[slot], waits=w)

        pre0 = {0: load_wgu(0), 1: load_wgu(1)}
        wd_tok = None
        for hc in range(NHC):
            w = [cast_tok[("wd", hc)]]
            if hc == 0 and state["arenaB_free"] is not None:
                w.extend(state["arenaB_free"])
            wd_tok = dma("sync", wdT_sb[:, hc * 1024:(hc + 1) * 1024],
                         s_wd[hc], s_wdT, waits=w)
        ln_tok = emit_ln(src_x, wait_store_tok=wait_src_store, tail_guard=tail_guard)
        h_done_halves = []
        for half in range(2):
            t0 = half * 1024
            pre = pre0 if half == 0 else {0: load_wgu(0), 1: load_wgu(1)}
            for hc in range(NHC):
                slot = hc % 2
                wtok = pre[hc] if hc in pre else load_wgu(hc)
                swg = s_wg[slot]
                for tc in range(2):
                    gbk, ubk = 2 * tc, 2 * tc + 1
                    rhs = LNT[:].rearrange("p (c t) -> p c t", c=DC)[
                        :, :, t0 + tc * 512: t0 + (tc + 1) * 512]
                    for c in range(DC):
                        wz = []
                        if c == 0:
                            wz = [(swg, wtok)] + bf_waits(gbk)
                            if hc == 0:
                                wz.append((s_vv, ln_tile_tok[(t0 + tc * 512 + 511) // 128]))
                        on("tensor", lambda t, c=c, gbk=gbk, rhs=rhs, slot=slot: t.matmul(
                            banks[gbk][:],
                            wgu_sb[:, slot * 2048 + c * 128: slot * 2048 + (c + 1) * 128],
                            rhs[:, c, :], start=(c == 0), stop=(c == DC - 1)),
                            waits=wz, inc=s_pe if c == DC - 1 else None)
                    g_tok = s_pe.n
                    for c in range(DC):
                        wz = []
                        if c == 0:
                            wz = bf_waits(ubk)
                        on("tensor", lambda t, c=c, ubk=ubk, rhs=rhs, slot=slot: t.matmul(
                            banks[ubk][:],
                            wgu_sb[:, slot * 2048 + 1024 + c * 128: slot * 2048 + 1024 + (c + 1) * 128],
                            rhs[:, c, :], start=(c == 0), stop=(c == DC - 1)),
                            waits=wz, inc=s_pe if c == DC - 1 else None)
                    u_tok = s_pe.n
                    sgv = sg_sb[:, tc * 512:(tc + 1) * 512]
                    w_sg = [(s_pe, g_tok)]
                    if sg_free[tc] is not None:
                        w_sg.append(sg_free[tc])
                    a_tok = on("scalar", lambda s, gbk=gbk, sgv=sgv: s.activation(
                        sgv, banks[gbk][:], AF.Silu, scale=QDEQ),
                        waits=w_sg, inc=s_aa)
                    bank_free[gbk] = [(s_aa, a_tok)]
                    hdst = H_view[:, hc * 1024 + tc * 512: hc * 1024 + (tc + 1) * 512]
                    v_tok = on("vector", lambda v, sgv=sgv, ubk=ubk, hdst=hdst: v.tensor_tensor(
                        hdst, sgv, banks[ubk][:], ALU.mult),
                        waits=[(s_pe, u_tok), (s_aa, a_tok)], inc=s_vv)
                    bank_free[ubk] = [(s_vv, v_tok)]
                    sg_free[tc] = (s_vv, v_tok)
                state["wgu_slot_free"][slot] = s_pe.n
            h_done = s_vv.n
            for ttl in range(8):
                tt = half * 8 + ttl
                sl = tt % 2
                w = [wait_src_store[tt]] if wait_src_store is not None else []
                if state["resid_free"][sl] is not None:
                    w.append(state["resid_free"][sl])
                r_tok = dma("sync", xrs_sb[:, sl * D:(sl + 1) * D],
                            src_x[tt * 128:(tt + 1) * 128, :], s_dr[sl], waits=w)
                e_tok = None
                for dc in range(2):
                    bk = (4 if ttl % 2 == 0 else 6) + dc
                    for hc in range(NHC):
                        wz = []
                        if hc == 0:
                            wz = [(s_wdT, wd_tok), (s_vv, h_done)] + bf_waits(bk)
                        on("tensor", lambda t, hc=hc, bk=bk, ttl=ttl, dc=dc: t.matmul(
                            banks[bk][:],
                            H_view[:, hc * 1024 + ttl * 128: hc * 1024 + (ttl + 1) * 128],
                            wdT_sb[:, hc * 1024 + dc * 512: hc * 1024 + (dc + 1) * 512],
                            start=(hc == 0), stop=(hc == NHC - 1)),
                            waits=wz, inc=s_pe if hc == NHC - 1 else None)
                    d_tok = s_pe.n
                    wz = [(s_pe, d_tok), (s_dr[sl], r_tok)]
                    if xpo_free[sl] is not None:
                        wz.append(xpo_free[sl])
                    if lnv_free[sl] is not None:
                        wz.append(lnv_free[sl])
                        if dc == 1:
                            lnv_free[sl] = None
                    e_tok = on("vector", lambda v, bk=bk, sl=sl, dc=dc: v.scalar_tensor_tensor(
                        xpo_sb[:, sl * D + dc * 512: sl * D + (dc + 1) * 512],
                        banks[bk][:], DOWN_SCALE,
                        xrs_sb[:, sl * D + dc * 512: sl * D + (dc + 1) * 512],
                        ALU.mult, ALU.add), waits=wz, inc=s_vv)
                    bank_free[bk] = [(s_vv, e_tok)]
                state["resid_free"][sl] = (s_vv, e_tok)
                st_tok = dma("sync", dst_x[tt * 128:(tt + 1) * 128, :],
                             xpo_sb[:, sl * D:(sl + 1) * D], s_st[sl],
                             waits=[(s_vv, e_tok)])
                xpo_free[sl] = (s_st[sl], st_tok)
                store_tile_tok[tt] = (s_st[sl], st_tok)
            h_done_halves.append(h_done)
        state["arenaB_free"] = [(s_pe, s_pe.n)]
        return list(store_tile_tok)


    OUT_BANKS = (6, 7, 2, 3)
    S_BANKS = (4, 5, 0, 1)
    OFF_V, OFF_SG, OFF_QT, OFF_KT = 0, 4128, 6176, 8224
    OFF_T1, OFF_T2, OFF_PP, OFF_OA = 10272, 12320, 14368, 15392
    OFF_WQK, OFF_WVG = 17456, 19504
    dsc = tabs["diag_scale"]

    def emit_attn(src_x, dst_x, wait_src_store):
        yT = arenaA
        VALL = arenaB[:, OFF_V:OFF_V + 4128]
        V3 = VALL.rearrange("p (hl z) -> p hl z", hl=2)
        SG = arenaB[:, OFF_SG:OFF_SG + 2048]
        QT = arenaB[:, OFF_QT:OFF_QT + 2048]
        KT = arenaB[:, OFF_KT:OFF_KT + 2048]
        T1 = arenaB[:, OFF_T1:OFF_T1 + 2048]
        T2 = arenaB[:, OFF_T2:OFF_T2 + 2048]
        PP = arenaB[:, OFF_PP:OFF_PP + 1024]
        OA = arenaB[:, OFF_OA:OFF_OA + 2064].bitcast(F32)
        OA3 = OA.rearrange("p (a b) -> p a b", b=129)
        WQK = arenaB[:, OFF_WQK:OFF_WQK + 2048]
        WVG = arenaB[:, OFF_WVG:OFF_WVG + 4096]
        junk3 = xrs_sb[:, 0:D].rearrange("p (a b) -> p a b", a=8)

        state["junk2_guard"] = [t for t in state["resid_free"] if t is not None]
        ln_tok = emit_ln(src_x, wait_store_tok=wait_src_store,
                         tail_guard=state["yT_read_done"])
        abf = state["arenaB_free"] or []
        t_ones = on("vector", lambda v: v.memset(VALL, 1.0),
                    waits=abf + [(s_dc2, consts2_tok)], inc=s_vv)

        oa_free = None
        gn_free = None
        sg_free2 = None
        last_mm2 = None
        last_ymul = None
        for pr in range(NH // 2):
            h0, h1 = 2 * pr, 2 * pr + 1
            w = [cast_tok[("qw", pr)]]
            if state["wqk_free"] is not None:
                w.append(state["wqk_free"])
            elif abf:
                w.extend(abf)
            tq = dma("sync", WQK[:, 0:1024], s_qw[pr], s_wq, waits=w)
            w2k = [cast_tok[("kw", pr)]]
            if state["wqk_free"] is not None:
                w2k.append(state["wqk_free"])
            tk = dma("sync", WQK[:, 1024:2048], s_kw[pr], s_wk, waits=w2k)
            w = [cast_tok[("vw", pr)]]
            if state["wvg_free"] is not None:
                w.append(state["wvg_free"])
            elif abf:
                w.extend(abf)
            tv = dma("sync", WVG[:, 0:2048], s_vw[pr], s_wv, waits=w)
            w2g = [cast_tok[("gw", pr)]]
            if state["wvg_free"] is not None:
                w2g.append(state["wvg_free"])
            tg = dma("sync", WVG[:, 2048:4096], s_gw[pr], s_wg2, waits=w2g)

            vev_toks = []
            for tt in range(NTT):
                bkv = tt % 2
                for c in range(DC):
                    wz = []
                    if c == 0:
                        wz = [(s_wv, tv)] + bf_waits(bkv)
                        if pr == 0 and tt < 2:
                            wz.append((s_vv, ln_tok))
                    on("tensor", lambda t, c=c, bkv=bkv, tt=tt: t.matmul(
                        banks[bkv][:, 0:256],
                        LNT[:, c * 2048 + tt * 128: c * 2048 + (tt + 1) * 128],
                        WVG[:, c * 256:(c + 1) * 256],
                        start=(c == 0), stop=(c == DC - 1)),
                        waits=wz, inc=s_pe if c == DC - 1 else None)
                v_tok = s_pe.n
                wz = [(s_pe, v_tok), (s_vv, t_ones)]
                t_ve = on("vector", lambda v, bkv=bkv, tt=tt, pr=pr: v.tensor_tensor(
                    V3[:, :, tt * 129: tt * 129 + 128],
                    banks[bkv][:, 0:256].rearrange("p (hl j) -> p hl j", hl=2),
                    vb_rep[:, pr * 256:(pr + 1) * 256].rearrange("p (hl j) -> p hl j", hl=2),
                    ALU.add), waits=wz, inc=s_vv)
                bank_free[bkv] = [(s_vv, t_ve)]
                vev_toks.append(t_ve)

            def qk_rotary(which, wtok, wsem, dstT, bias_ap, t2_guard=None):
                toks = {}
                proj_tok = {}

                def proj(tc):
                    bkq = 2 + tc % 2
                    wz0 = [(wsem, wtok)] + bf_waits(bkq)
                    if pr == 0 and tc < 2 and which == 0:
                        wz0.append((s_vv, ln_tok))
                    if pr == 0 and tc == 0 and which == 0:
                        wz0.append((s_dc, consts_tok))
                    for c in range(DC):
                        on("tensor", lambda t, c=c, bkq=bkq, tc=tc, which=which: t.matmul(
                            banks[bkq][:],
                            WQK[:, which * 1024 + c * 128: which * 1024 + (c + 1) * 128],
                            LNT[:].rearrange("p (c t) -> p c t", c=DC)[:, c, tc * 512:(tc + 1) * 512],
                            start=(c == 0), stop=(c == DC - 1)),
                            waits=wz0 if c == 0 else (), inc=s_pe if c == DC - 1 else None)
                    proj_tok[tc] = s_pe.n

                def rot(tc):
                    bkq = 2 + tc % 2
                    p_tok = proj_tok[tc]
                    w_ev = [(s_pe, p_tok)]
                    if tc == 0 and t2_guard is not None:
                        w_ev.append(t2_guard)
                    t_ev = on("scalar", lambda s, bkq=bkq, tc=tc, dstT=dstT, bias_ap=bias_ap: s.activation(
                        dstT[:, tc * 512:(tc + 1) * 512], banks[bkq][:],
                        AF.Identity, bias=bias_ap), waits=w_ev, inc=s_aa)
                    bank_free[bkq] = [(s_aa, t_ev)]
                    bkr = 4 + tc % 2
                    wz = [(s_aa, t_ev)] + bf_waits(bkr)
                    on("tensor", lambda t, bkr=bkr, tc=tc, dstT=dstT: t.matmul(
                        banks[bkr][0:64, :], permT[0:64, :],
                        dstT[0:64, tc * 512:(tc + 1) * 512],
                        start=True, stop=True, tile_position=(0, 0)),
                        waits=wz)
                    r_tok = on("tensor", lambda t, bkr=bkr, tc=tc, dstT=dstT: t.matmul(
                        banks[bkr][64:128, :], permT[64:128, :],
                        dstT[64:128, tc * 512:(tc + 1) * 512],
                        start=True, stop=True, tile_position=(64, 64)), inc=s_pe)
                    t_1 = on("vector", lambda v, bkr=bkr, tc=tc: v.tensor_mul(
                        T1[:, tc * 512:(tc + 1) * 512], banks[bkr][:],
                        sinT[:, tc * 512:(tc + 1) * 512]),
                        waits=[(s_pe, r_tok)], inc=s_vv)
                    t_2 = on("gpsimd", lambda g, tc=tc, dstT=dstT: g.tensor_mul(
                        T2[:, tc * 512:(tc + 1) * 512],
                        dstT[:, tc * 512:(tc + 1) * 512],
                        cosT[:, tc * 512:(tc + 1) * 512]),
                        waits=[(s_aa, t_ev), gn_free or (None, 0)] if tc == 0 else [(s_aa, t_ev)],
                        inc=s_gp)
                    t_c = on("vector", lambda v, tc=tc, dstT=dstT: v.tensor_add(
                        dstT[:, tc * 512:(tc + 1) * 512],
                        T1[:, tc * 512:(tc + 1) * 512],
                        T2[:, tc * 512:(tc + 1) * 512]),
                        waits=[(s_vv, t_1), (s_gp, t_2), (s_pe, r_tok)], inc=s_vv)
                    bank_free[bkr] = [(s_vv, t_1)]
                    toks[tc] = t_c

                proj(0); proj(1); rot(0); proj(2); rot(1); proj(3); rot(2); rot(3)
                return toks[3], s_pe.n

            qdone, _ = qk_rotary(0, tq, s_wq, QT, bias_q[:, pr:pr + 1],
                                 state.get("rot_t2_guard"))
            kdone, qk_mm_tok = qk_rotary(1, tk, s_wk, KT, bias_k[:, pr:pr + 1],
                                         (s_vv, qdone))
            state["rot_t2_guard"] = (s_vv, kdone)
            state["wqk_free"] = (s_pe, qk_mm_tok)

            for hl, h in ((0, h0), (1, h1)):
                wz_sg = [sg_free2] if sg_free2 is not None else []
                for tc in range(4):
                    bkg = 2 + tc % 2
                    for c in range(DC):
                        wz = []
                        if c == 0:
                            wz = [(s_wg2, tg)] + bf_waits(bkg)
                        on("tensor", lambda t, c=c, bkg=bkg, tc=tc, hl=hl: t.matmul(
                            banks[bkg][:],
                            WVG[:, 2048 + c * 256 + hl * 128: 2048 + c * 256 + hl * 128 + 128],
                            LNT[:].rearrange("p (c t) -> p c t", c=DC)[:, c, tc * 512:(tc + 1) * 512],
                            start=(c == 0), stop=(c == DC - 1)),
                            waits=wz, inc=s_pe if c == DC - 1 else None)
                    g_tok = s_pe.n
                    t_sg = on("scalar", lambda s, bkg=bkg, tc=tc, h=h: s.activation(
                        SG[:, tc * 512:(tc + 1) * 512], banks[bkg][:],
                        AF.Silu, bias=gb_f[:, h:h + 1]),
                        waits=[(s_pe, g_tok)] + wz_sg, inc=s_aa)
                    bank_free[bkg] = [(s_aa, t_sg)]
                sg_tok = s_aa.n
                last_g_mm = s_pe.n

                pp_free = [None, None]
                gnw = []
                for mb in range(4):
                    half = mb // 2
                    out_evacs = []
                    ntmax = 4 * mb + 3
                    mm1tok = {}

                    def emit_mm1(nt, mb=mb, hl=hl):
                        bks = S_BANKS[nt % 4]
                        wz = bf_waits(bks)
                        if nt == 0 and mb == 0:
                            wz = wz + [(s_vv, qdone), (s_vv, kdone),
                                       (s_vv, vev_toks[-1])]
                        ms0_ = max(0, nt - 4 * mb)
                        mm1tok[nt] = on("tensor", lambda t, bks=bks, nt=nt, mb=mb, hl=hl, ms0_=ms0_: t.matmul(
                            banks[bks][:, ms0_ * 128:512],
                            KT[hl * 64:(hl + 1) * 64, nt * 128:(nt + 1) * 128],
                            QT[hl * 64:(hl + 1) * 64, mb * 512 + ms0_ * 128:(mb + 1) * 512],
                            start=True, stop=True, tile_position=(hl * 64, 0)),
                            waits=wz, inc=s_pe)

                    for nt in range(min(MM1_LOOKAHEAD, ntmax + 1)):
                        emit_mm1(nt)
                    for nt in range(ntmax + 1):
                        bks = S_BANKS[nt % 4]
                        if MM1_LOOKAHEAD == 0 and nt == 0:
                            emit_mm1(0)
                        mm1 = mm1tok[nt]
                        slot = nt % 2
                        ms_d = nt - 4 * mb
                        wz_e = [(s_pe, mm1)]
                        if pp_free[slot] is not None:
                            wz_e.extend(pp_free[slot])
                        toks_e = []
                        if 0 <= ms_d <= 3:
                            toks_e.append(("vector", on("vector",
                                lambda v, bks=bks, slot=slot, ms_d=ms_d, h=h, nt=nt: v.scalar_tensor_tensor(
                                    PP[:, slot * 512 + ms_d * 128: slot * 512 + (ms_d + 1) * 128],
                                    banks[bks][:, ms_d * 128:(ms_d + 1) * 128],
                                    float(dsc[h][nt]),
                                    tril2[:, h * 128:(h + 1) * 128],
                                    ALU.mult, ALU.mult),
                                waits=wz_e, inc=s_vv)))
                        ms0 = max(0, ms_d + 1)
                        if ms0 <= 3:
                            toks_e.append(("scalar", on("scalar",
                                lambda s, bks=bks, slot=slot, ms0=ms0, h=h, nt=nt: s.activation(
                                    PP[:, slot * 512 + ms0 * 128: slot * 512 + 512],
                                    banks[bks][:, ms0 * 128:512],
                                    AF.Copy, scale=bg3[:, h * 16 + nt: h * 16 + nt + 1]),
                                waits=wz_e, inc=s_aa)))
                        bank_free[bks] = [((s_vv, t) if e == "vector" else (s_aa, t))
                                          for (e, t) in toks_e]
                        nxt = nt + (MM1_LOOKAHEAD if MM1_LOOKAHEAD else 1)
                        if nxt <= ntmax and nxt not in mm1tok:
                            emit_mm1(nxt)
                        wz_m = list(bank_free[bks])
                        first = True
                        msl = list(range(max(0, ms_d), 4))
                        for ms in msl:
                            msub = 4 * mb + ms
                            ob = OUT_BANKS[ms]
                            wzm = wz_m if first else []
                            if nt == 0 and first:
                                for b_ in OUT_BANKS:
                                    wzm = wzm + bf_waits(b_)
                            first = False
                            stop = (nt == 4 * mb + ms)
                            do_inc = stop or (ms == msl[-1])
                            last = on("tensor", lambda t, ob=ob, ms=ms, slot=slot, nt=nt, hl=hl, mb=mb, stop=stop: t.matmul(
                                banks[ob][:, 0:129],
                                PP[:, slot * 512 + ms * 128: slot * 512 + (ms + 1) * 128],
                                V3[:, hl, nt * 129:(nt + 1) * 129],
                                start=(nt == 0), stop=stop),
                                waits=wzm,
                                inc=s_pe if do_inc else None)
                            if stop:
                                out_evacs.append((ms, last))
                        pp_free[slot] = [(s_pe, s_pe.n)]
                    mtl0 = (mb % 2) * 4
                    done = {ms: t for (ms, t) in out_evacs}
                    oa_w = list(oa_free) if oa_free is not None else []
                    ev_toks = []
                    for ms in range(4):
                        ob = OUT_BANKS[ms]
                        acol = alpha[:, h * 16 + 4 * mb + ms: h * 16 + 4 * mb + ms + 1]
                        wz = [(s_pe, done[ms])] + (oa_w if ms == 0 else [])
                        if ms % 2 == 0:
                            t_o = on("vector", lambda v, ob=ob, mtl0=mtl0, ms=ms, acol=acol: v.tensor_scalar(
                                OA[:, (mtl0 + ms) * 129:(mtl0 + ms + 1) * 129],
                                banks[ob][:, 0:129], acol, None, ALU.mult),
                                waits=wz, inc=s_vv)
                            bank_free[ob] = [(s_vv, t_o)]
                            ev_toks.append(("vector", t_o))
                        else:
                            t_o = on("scalar", lambda s, ob=ob, mtl0=mtl0, ms=ms, acol=acol: s.activation(
                                OA[:, (mtl0 + ms) * 129:(mtl0 + ms + 1) * 129],
                                banks[ob][:, 0:129], AF.Copy, scale=acol),
                                waits=wz, inc=s_aa)
                            bank_free[ob] = [(s_aa, t_o)]
                            ev_toks.append(("scalar", t_o))
                    t_o1 = [t for (e, t) in ev_toks if e == "vector"][-1]
                    t_o2 = [t for (e, t) in ev_toks if e == "scalar"][-1]
                    if mb % 2 == 0:
                        continue
                    evw = [(s_vv, t_o1), (s_aa, t_o2)]
                    e1 = on("vector", lambda v: v.tensor_copy(
                        ep_rs[:, 0:8], OA3[:, :, 128:129]), waits=evw, inc=s_vv)
                    e7 = on("vector", lambda v: v.reduce_sum(
                        ep_m1[:, 0:8], OA3[:, :, 0:128], axis=AX.X), inc=s_vv)
                    e2 = on("scalar", lambda s: s.activation(
                        ep_d[:, 0:8], ep_rs[:, 0:8], AF.Abs),
                        waits=[(s_vv, e1)], inc=s_aa)
                    e4 = on("vector", lambda v: v.tensor_scalar(
                        ep_d[:, 0:8], ep_d[:, 0:8], 1.0, None, ALU.max),
                        waits=[(s_aa, e2)], inc=s_vv)
                    e5 = on("vector", lambda v: v.scalar_tensor_tensor(
                        ep_beta[:, 0:8], ep_d[:, 0:8], EPS, ep_d[:, 0:8],
                        ALU.mult, ALU.mult), waits=[(s_vv, e4)], inc=s_vv)
                    w8 = list(evw)
                    if state["junk2_guard"] is not None:
                        w8.extend(state["junk2_guard"])
                        state["junk2_guard"] = None
                    e8 = on("scalar", lambda s: s.activation(
                        junk3, OA3[:, :, 0:128], AF.Square), waits=w8, inc=s_aa)
                    state["junk2_free"] = (s_aa, e8)
                    e9 = on("vector", lambda v: v.reduce_sum(
                        ep_m2[:, 0:8], junk3, axis=AX.X), waits=[(s_aa, e8)], inc=s_vv)
                    e10 = on("vector", lambda v: v.scalar_tensor_tensor(
                        ep_rs[:, 0:8], ep_m1[:, 0:8], 1.0 / 128, ep_m1[:, 0:8],
                        ALU.mult, ALU.mult), waits=[(s_vv, e7)], inc=s_vv)
                    e11 = on("vector", lambda v: v.tensor_sub(
                        ep_sc[:, 0:8], ep_m2[:, 0:8], ep_rs[:, 0:8]),
                        waits=[(s_vv, e10), (s_vv, e9)], inc=s_vv)
                    e11b = on("vector", lambda v: v.scalar_tensor_tensor(
                        ep_bi[:, 0:8], ep_sc[:, 0:8], 1.0 / 128, ep_beta[:, 0:8],
                        ALU.mult, ALU.add), waits=[(s_vv, e11), (s_vv, e5)], inc=s_vv)
                    e12 = on("scalar", lambda s: s.activation(
                        ep_m2[:, 0:8], ep_bi[:, 0:8], AF.Ln),
                        waits=[(s_vv, e11b)], inc=s_aa)
                    e13 = on("scalar", lambda s: s.activation(
                        ep_m2[:, 0:8], ep_m2[:, 0:8], AF.Exp, scale=-0.5),
                        waits=[(s_aa, e12)], inc=s_aa)
                    e15 = on("vector", lambda v: v.scalar_tensor_tensor(
                        ep_bi[:, 0:8], ep_m1[:, 0:8], -1.0 / 128, ep_m2[:, 0:8],
                        ALU.mult, ALU.mult), waits=[(s_aa, e13)], inc=s_vv)
                    lastf = None
                    for mtl in range(8):
                        wz = [(s_vv, e15)] if mtl == 0 else []
                        if mtl == 0 and gn_free is not None:
                            wz.append(gn_free)
                        lastf = on("vector", lambda v, mtl=mtl, half=half: v.tensor_scalar(
                            T2[:, (half * 8 + mtl) * 128:(half * 8 + mtl + 1) * 128],
                            OA3[:, mtl, 0:128],
                            ep_m2[:, mtl:mtl + 1], ep_bi[:, mtl:mtl + 1],
                            ALU.mult, ALU.add), waits=wz, inc=s_vv)
                    oa_free = [(s_vv, lastf), (s_aa, e8)]
                    gnw.append(lastf)
                for grp in range(2):
                    bk = grp
                    tokp = None
                    for j in range(8):
                        mt = grp * 8 + j
                        wz = []
                        if j == 0:
                            wz = bf_waits(bk) + [(s_vv, gnw[grp])]
                            if grp == 0:
                                wz.append((s_aa, sg_tok))
                        tokp = on("tensor", lambda t, bk=bk, mt=mt: t.transpose(
                            bank_bf16(bk)[:, (mt % 8) * 128:((mt % 8) + 1) * 128],
                            T2[:, mt * 128:(mt + 1) * 128], ident[:]),
                            waits=wz, inc=s_pe if j == 7 else None)
                    wy = [(s_pe, tokp)]
                    if last_ymul is None and pr == 0:
                        wy.extend(cast_done)
                    t_y = on("vector", lambda v, bk=bk, grp=grp, h=h: v.tensor_mul(
                        yT[:, h * 2048 + grp * 1024: h * 2048 + (grp + 1) * 1024],
                        bank_bf16(bk)[:],
                        SG[:, grp * 1024:(grp + 1) * 1024]),
                        waits=wy, inc=s_vv)
                    bank_free[bk] = [(s_vv, t_y)]
                    last_ymul = t_y
                gn_free = (s_pe, s_pe.n)
                sg_free2 = (s_vv, last_ymul)
            state["wvg_free"] = (s_pe, last_g_mm)
        proj_done = (s_pe, s_pe.n)
        for oc4 in range(4):
            w = [proj_done, cast_tok[("ow", oc4 * 4)], cast_tok[("ow", oc4 * 4 + 1)],
                 cast_tok[("ow", oc4 * 4 + 2)], cast_tok[("ow", oc4 * 4 + 3)]]
            dma("sync",
                LNT[:, oc4 * 4096:(oc4 + 1) * 4096].rearrange("p (a b) -> p a b", a=4),
                s_ow_d[oc4 * 4:oc4 * 4 + 4].transpose([1, 0, 2]), s_ow, waits=w)
        ow_tok = s_ow.n
        for tt in range(NTT):
            sl = tt % 2
            w = [wait_src_store[tt]] if wait_src_store is not None else []
            if state["resid_free"][sl] is not None:
                w.append(state["resid_free"][sl])
            if tt == 0 and state["junk2_free"] is not None:
                w.append(state["junk2_free"])
                state["junk2_free"] = None
            r_tok = dma("sync", xrs_sb[:, sl * D:(sl + 1) * D],
                        src_x[tt * 128:(tt + 1) * 128, :], s_dr[sl], waits=w)
            e_tok = None
            for dc in range(2):
                bk = (2 if tt % 2 == 0 else 4) + dc
                for fc in range(NH):
                    wz = []
                    if fc == 0:
                        wz = bf_waits(bk) + [(s_ow, ow_tok)]
                        if tt == 0 and dc == 0:
                            wz.append((s_vv, last_ymul))
                    on("tensor", lambda t, fc=fc, bk=bk, tt=tt, dc=dc: t.matmul(
                        banks[bk][:],
                        yT[:, fc * 2048 + tt * 128: fc * 2048 + (tt + 1) * 128],
                        LNT[:, fc * 1024 + dc * 512: fc * 1024 + (dc + 1) * 512],
                        start=(fc == 0), stop=(fc == NH - 1)),
                        waits=wz, inc=s_pe if fc == NH - 1 else None)
                o_tok = s_pe.n
                wz = [(s_pe, o_tok)]
                if xpo_free[sl] is not None:
                    wz.append(xpo_free[sl])
                if lnv_free[sl] is not None:
                    wz.append(lnv_free[sl])
                    if dc == 1:
                        lnv_free[sl] = None
                if tt == 0 and dc == 0:
                    wz.append((s_dc2, consts2_tok))
                t_b1 = on("vector", lambda v, bk=bk, sl=sl, dc=dc: v.tensor_tensor(
                    xpo_sb[:, sl * D + dc * 512: sl * D + (dc + 1) * 512],
                    banks[bk][:], ob_rep[:, dc * 512:(dc + 1) * 512], ALU.add),
                    waits=wz, inc=s_vv)
                e_tok = on("vector", lambda v, sl=sl, dc=dc: v.tensor_add(
                    xpo_sb[:, sl * D + dc * 512: sl * D + (dc + 1) * 512],
                    xpo_sb[:, sl * D + dc * 512: sl * D + (dc + 1) * 512],
                    xrs_sb[:, sl * D + dc * 512: sl * D + (dc + 1) * 512]),
                    waits=[(s_dr[sl], r_tok), (s_vv, t_b1)], inc=s_vv)
                bank_free[bk] = [(s_vv, e_tok)]
            state["resid_free"][sl] = (s_vv, e_tok)
            st_tok = dma("sync", dst_x[tt * 128:(tt + 1) * 128, :],
                         xpo_sb[:, sl * D:(sl + 1) * D], s_st[sl],
                         waits=[(s_vv, e_tok)])
            xpo_free[sl] = (s_st[sl], st_tok)
            store_tile_tok[tt] = (s_st[sl], st_tok)
        state["yT_read_done"] = (s_pe, s_pe.n)
        state["arenaB_free"] = [(s_pe, s_pe.n), (s_vv, s_vv.n),
                                (s_aa, s_aa.n), (s_gp, s_gp.n)]
        return list(store_tile_tok)

    for b in range(B):
        A, S, O = x_in[b], x_scr[b], x_out[b]
        if N_SUBLAYERS == 1:
            emit_ffn(A, O, wait_src_store=None)
        elif N_SUBLAYERS == 2:
            tok1 = emit_ffn(A, S, wait_src_store=None)
            emit_attn(S, O, wait_src_store=tok1)
        else:
            tok1 = emit_ffn(A, S, wait_src_store=None)
            tok2 = emit_attn(S, O, wait_src_store=tok1)
            emit_ffn(O, O, wait_src_store=tok2,
                     tail_guard=state["yT_read_done"])

    P.emit("sync", lambda e, n0=s_st[0].n, n1=s_st[1].n: (
        e.wait_ge(s_st[0].h, n0), e.wait_ge(s_st[1].h, n1)))

    with nc.Block() as block:
        P.realize(block)
    stack.close()
    return nc



def kernel(**inputs):
    tabs = host_tables()
    wts = retile_weights(inputs)
    x = np.asarray(inputs["L0_attn_x"])

    nc = build_program(tabs)

    in_maps = []
    for c in range(NCORES):
        in_maps.append({
            "x_in": np.ascontiguousarray(x[c * B:(c + 1) * B]),
            "wgu_t": wts["wgu_t"], "wd_t": wts["wd_t"],
            "qw_t": wts["qw_t"], "kw_t": wts["kw_t"],
            "vw_t": wts["vw_t"], "gw_t": wts["gw_t"], "ow_t": wts["ow_t"],
            "q_b": wts["q_b"], "k_b": wts["k_b"], "v_b": wts["v_b"],
            "g_b": wts["g_b"], "o_b": wts["o_b"],
            "c_sinT": tabs["sinT"], "c_cosT": tabs["cosT"],
            "c_permT": tabs["permT"], "c_ident": tabs["ident"],
            "c_identf": tabs["identf"],
            "c_bg3": tabs["bg3"], "c_tril2": tabs["tril2"],
            "c_alpha": tabs["alpha"],
        })

    res = run_bass_kernel_spmd(nc, in_maps, core_ids=list(range(NCORES)))
    return np.concatenate([r["x_out"] for r in res.results], axis=0)
